# version 12
# baseline (speedup 1.0000x reference)
import os
os.environ.setdefault("BASS_INTERP_NUM_WORKERS", "8")

import contextlib
import numpy as np

import concourse.bass as bass
import concourse.mybir as mybir
from concourse.bass_utils import run_bass_kernel_spmd

F32 = mybir.dt.float32
BF16 = mybir.dt.bfloat16
A = mybir.ActivationFunctionType
ALU = mybir.AluOpType
AX = mybir.AxisListType

D = 2048
NE = D // 128
INTER = 8192
NIC = INTER // 128
SEQ = 16384
NCORES = 8
EPS = 1e-5


class KB:

    ENGS = ("tensor", "scalar", "vector", "gpsimd", "sync")

    def __init__(self):
        self.recs = {e: [] for e in self.ENGS}
        self.sem_names = []
        self.total = {}
        self.selfsem = {"tensor": "MM", "scalar": "SC", "vector": "VV"}
        for s in ("MM", "SC", "VV"):
            self._reg(s)

    def _reg(self, name):
        if name not in self.total:
            self.sem_names.append(name)
            self.total[name] = 0

    def sem(self, name):
        self._reg(name)
        return name

    def op(self, eng, fn, waits=(), dma=None, inc=None, notok=False):
        waits = [w for w in waits if w is not None]
        if notok:
            self.recs[eng].append((fn, waits, None))
            return None
        if dma is not None:
            self._reg(dma)
            self.total[dma] += 16
            tok = (dma, self.total[dma])
            self.recs[eng].append((fn, waits, (dma, 16)))
            return tok
        if inc is not None:
            s, n = inc
            self._reg(s)
            self.total[s] += n
            tok = (s, self.total[s])
            self.recs[eng].append((fn, waits, (s, n)))
            return tok
        if eng in self.selfsem:
            s = self.selfsem[eng]
            self.total[s] += 1
            tok = (s, self.total[s])
            self.recs[eng].append((fn, waits, (s, 1)))
            return tok
        self.recs[eng].append((fn, waits, None))
        return None

    def wait(self, eng, toks):
        toks = [t for t in toks if t is not None]
        if toks:
            self.recs[eng].append((None, toks, None))

    def snapshot(self):
        return dict(self.total)

    def barrier(self, snap=None, exclude=()):
        snap = snap or self.snapshot()
        toks = [(s, v) for s, v in snap.items() if v > 0 and s not in exclude]
        for e in self.ENGS:
            self.wait(e, toks)

    def replay(self, eng_name, eng, sems):
        waited = {}
        for fn, waits, inc in self.recs[eng_name]:
            for (s, v) in waits:
                if v > 0 and waited.get(s, -1) < v:
                    eng.wait_ge(sems[s], v)
                    waited[s] = v
            if fn is not None:
                instr = fn(eng, sems)
                if inc is not None and instr is not None:
                    instr.then_inc(sems[inc[0]], inc[1])


def build_nc(T):
    NJ = T // 128
    TT = min(512, T)
    NT4 = T // TT
    TQ = min(256, T)
    NT8 = T // TQ

    nc = bass.Bass("TRN2", target_bir_lowering=False)

    x_h = nc.dram_tensor("x", [T, D], F32, kind="ExternalInput").ap()
    w_in_h = nc.dram_tensor("w_in_r", [NE, 128, NE, 128], F32, kind="ExternalInput").ap()
    w_dec_h = nc.dram_tensor("w_dec_r", [NE, 128, NE, 128], F32, kind="ExternalInput").ap()
    w_gate_h = nc.dram_tensor("w_gate_r", [NE, 128, NE, 128], F32, kind="ExternalInput").ap()
    w_out_h = nc.dram_tensor("w_out", [D, D], F32, kind="ExternalInput").ap()
    w_gu_h = nc.dram_tensor("w_gu_r", [2 * NIC, 128, NE, 128], F32, kind="ExternalInput").ap()
    w_dn_h = nc.dram_tensor("w_dn_r", [4, 4, 128, 16, 512], F32, kind="ExternalInput").ap()
    b_in_h = nc.dram_tensor("b_in_t", [128, NE], F32, kind="ExternalInput").ap()
    b_dec_h = nc.dram_tensor("b_dec_t", [128, NE], F32, kind="ExternalInput").ap()
    b_db_h = nc.dram_tensor("b_db_t", [128, NE], F32, kind="ExternalInput").ap()
    b_gate_h = nc.dram_tensor("b_gate_t", [128, NE], F32, kind="ExternalInput").ap()
    b_out_h = nc.dram_tensor("b_out_t", [128, NE], F32, kind="ExternalInput").ap()
    eye_h = nc.dram_tensor("eye32", [128, 128], F32, kind="ExternalInput").ap()
    mask_h = nc.dram_tensor("cmask", [128, NCORES], F32, kind="ExternalInput").ap()
    out_h = nc.dram_tensor("out", [T, D], F32, kind="ExternalOutput").ap()

    sc_d = nc.dram_tensor("sc_d", [NE, 128, T], F32, kind="Internal").ap()
    sc_bp = nc.dram_tensor("sc_bp", [NE, 128, T], F32, kind="Internal").ap()
    sc_gate = nc.dram_tensor("sc_gate", [NE, 128, T], F32, kind="Internal").ap()
    sc_g = nc.dram_tensor("sc_g", [NE, 128, T], F32, kind="Internal").ap()
    sc_h = nc.dram_tensor("sc_h", [NIC, 128, T], BF16, kind="Internal").ap()
    cc_in = nc.dram_tensor("cc_in", [128, 2 * NE], F32, kind="Internal").ap()
    cc_out = nc.dram_tensor("cc_out", [NCORES, 128, 2 * NE], F32, kind="Internal").ap()
    sc_wdn = nc.dram_tensor("sc_wdn", [4, 4, 128, 16, 512], BF16, kind="Internal").ap()
    sc_wout = nc.dram_tensor("sc_wout", [NE, 128, D], BF16, kind="Internal").ap()

    b = KB()
    es = contextlib.ExitStack()
    sbufs = {}

    def sb(name, shape, dtype=F32):
        sbufs[name] = es.enter_context(nc.sbuf_tensor(name, shape, dtype))
        return sbufs[name]

    bi_t = sb("bi_t", [128, NE])
    bdc_t = sb("bdc_t", [128, NE])
    bdb_t = sb("bdb_t", [128, NE])
    bg_t = sb("bg_t", [128, NE])
    bo_t = sb("bo_t", [128, NE])
    eye32 = sb("eye32s", [128, 128])
    eye16 = sb("eye16s", [128, 128], BF16)
    cmask = sb("cmasks", [128, NCORES])
    pair = sb("pair", [128, 2 * NE])
    gsb = sb("gsb", [128, NCORES, 2 * NE])
    hcomb = sb("hcomb", [128, NE])
    tmp1 = sb("tmp1", [128, NE])
    tmp2 = sb("tmp2", [128, NE])
    carry = sb("carry", [128, NE])

    pb = [es.enter_context(nc.psum_tensor(f"pb{i}", [128, 512], F32)) for i in range(4)]
    pb16 = [es.enter_context(nc.psum_tensor(f"pb16_{i}", [128, 1024], BF16)) for i in range(2)]
    pb32 = [es.enter_context(nc.psum_tensor(f"pb32_{i}", [128, 512], F32)) for i in range(2)]
    banks6 = pb + pb32
    bank_free = [None] * 6
    tb16_free = [None] * 2
    ring = [0]

    def ring_next():
        bk = ring[0] % 6
        ring[0] += 1
        return bk

    xn_stack = contextlib.ExitStack()
    xn = xn_stack.enter_context(nc.sbuf_tensor("xn", [128, NE, T], BF16))

    L0 = b.sem("L0")
    for dst, src in ((bi_t, b_in_h), (bdc_t, b_dec_h), (bdb_t, b_db_h),
                     (bg_t, b_gate_h), (bo_t, b_out_h), (eye32, eye_h),
                     (cmask, mask_h)):
        b.op("sync", lambda e, s, dst=dst, src=src: e.dma_start(dst[:], src[:, :]),
             dma="L0")
    l0_tok = ("L0", b.total["L0"])
    t_eye16 = b.op("scalar", lambda e, s: e.copy(eye16[:], eye32[:]), waits=[l0_tok])
    t_bdc = b.op("vector", lambda e, s: e.tensor_add(bdc_t[:], bdc_t[:], bdb_t[:]),
                 waits=[l0_tok])

    p1 = contextlib.ExitStack()

    def sb1(name, shape, dtype=F32):
        sbufs[name] = p1.enter_context(nc.sbuf_tensor(name, shape, dtype))
        return sbufs[name]

    NSL = 4
    x_t = [sb1(f"x_t{i}", [128, D]) for i in range(NSL)]
    xn_tm = [sb1(f"xn_tm{i}", [128, D], BF16) for i in range(NSL)]
    st2 = [{k: sb1(f"st{i}_" + k, [128, 1]) for k in
            ("sx", "sq", "mean", "msq", "m2", "vv1", "vp", "s1", "r0",
             "aa", "bb", "cc", "rr", "nb")} for i in range(NSL)]
    xsqs = [sb1(f"xsqs{i}", [128, D]) for i in range(NSL)]

    STGL = [b.sem("STGL0"), b.sem("STGL1")]
    STGS = [b.sem("STGS0"), b.sem("STGS1")]
    precast_jobs = []
    for q in range(4):
        for part in range(4):
            precast_jobs.append((
                lambda sl, q=q, part=part: (stg_w[sl][:, :, :], w_dn_h[q, part, :, :, :]),
                lambda sl, q=q, part=part: (sc_wdn[q, part, :, :, :], stg_w[sl][:, :, :])))
    for k in range(NE):
        precast_jobs.append((
            lambda sl, k=k: (stg_w[sl][:, 0:4, :].rearrange("p a c -> p (a c)"),
                             w_out_h[k * 128:(k + 1) * 128, :]),
            lambda sl, k=k: (sc_wout[k, :, :],
                             stg_w[sl][:, 0:4, :].rearrange("p a c -> p (a c)"))))
    pc_i = [0]
    pc_store_tok = [None, None]
    xn_t4_tok = {}

    def emit_precast(njobs):
        for _ in range(njobs):
            i = pc_i[0]
            if i >= len(precast_jobs): return
            pc_i[0] += 1
            sl = i % 2
            ldfn, stfn = precast_jobs[i]
            t_l = b.op("gpsimd",
                       lambda e, s, ldfn=ldfn, sl=sl: e.dma_start(*ldfn(sl)),
                       waits=[pc_store_tok[sl]], dma=STGL[sl])
            pc_store_tok[sl] = b.op("sync",
                                    lambda e, s, stfn=stfn, sl=sl: e.dma_start(*stfn(sl)),
                                    waits=[t_l], dma=STGS[sl])

    LX = [b.sem(f"LX{i}") for i in range(NSL)]
    xslot_consumers = [[] for _ in range(NSL)]
    xnslot_consumers = [[] for _ in range(NSL)]
    xsq_consumer = [None] * NSL
    prev_stats = [None] * NSL

    for j in range(NJ):
        sl = j % NSL
        t_load = b.op(
            "sync",
            lambda e, s, sl=sl, j=j: e.dma_start(x_t[sl][:], x_h[j * 128:(j + 1) * 128, :]),
            waits=xslot_consumers[sl], dma=LX[sl])
        xslot_consumers[sl] = []

        st_ = st2[sl]
        myxsq = xsqs[sl]
        t_rs1 = b.op("scalar",
                     lambda e, s, sl=sl, myxsq=myxsq, st_=st_: e.activation(
                         myxsq[:], x_t[sl][:], A.Identity, accum_out=st_["sx"][:]),
                     waits=[t_load, prev_stats[sl], xsq_consumer[sl]])
        t_rs2 = b.op("scalar",
                     lambda e, s, sl=sl, myxsq=myxsq, st_=st_: e.activation(
                         myxsq[:], x_t[sl][:], A.Square, accum_out=st_["sq"][:]),
                     waits=[t_rs1])
        t_sq = t_rs2
        xsq_consumer[sl] = t_rs2
        inv = 1.0 / D
        t_m = b.op("vector", lambda e, s, st_=st_: e.tensor_scalar_mul(st_["mean"][:], st_["sx"][:], inv),
                   waits=[t_rs1])
        t_mq = b.op("vector", lambda e, s, st_=st_: e.tensor_scalar_mul(st_["msq"][:], st_["sq"][:], inv),
                    waits=[t_rs2])
        t_m2 = b.op("vector", lambda e, s, st_=st_: e.tensor_mul(st_["m2"][:], st_["mean"][:], st_["mean"][:]),
                    waits=[t_m])
        t_v1 = b.op("vector", lambda e, s, st_=st_: e.tensor_sub(st_["vv1"][:], st_["msq"][:], st_["m2"][:]),
                    waits=[t_mq, t_m2])
        t_vp = b.op("vector", lambda e, s, st_=st_: e.tensor_scalar_add(st_["vp"][:], st_["vv1"][:], EPS),
                    waits=[t_v1])
        t_s1 = b.op("scalar", lambda e, s, st_=st_: e.activation(st_["s1"][:], st_["vp"][:], A.Sqrt),
                    waits=[t_vp])
        t_r0 = b.op("vector", lambda e, s, st_=st_: e.reciprocal(st_["r0"][:], st_["s1"][:]),
                    waits=[t_s1])
        t_aa = b.op("vector", lambda e, s, st_=st_: e.tensor_mul(st_["aa"][:], st_["r0"][:], st_["r0"][:]),
                    waits=[t_r0])
        t_bb = b.op("vector", lambda e, s, st_=st_: e.tensor_mul(st_["bb"][:], st_["aa"][:], st_["vp"][:]),
                    waits=[t_aa, t_vp])
        t_cc = b.op("vector", lambda e, s, st_=st_: e.tensor_scalar(st_["cc"][:], st_["bb"][:], -0.5, 1.5,
                                                           ALU.mult, ALU.add),
                    waits=[t_bb])
        t_rr = b.op("vector", lambda e, s, st_=st_: e.tensor_mul(st_["rr"][:], st_["r0"][:], st_["cc"][:]),
                    waits=[t_cc])
        t_nb = b.op("vector", lambda e, s, st_=st_: e.tensor_scalar(st_["nb"][:], st_["mean"][:],
                                                           st_["rr"][:, 0:1], -1.0,
                                                           ALU.mult, ALU.mult),
                    waits=[t_rr])
        t_xn = b.op("scalar",
                    lambda e, s, sl=sl, st_=st_: e.activation(xn_tm[sl][:], x_t[sl][:], A.Identity,
                                                              bias=st_["nb"][:, 0:1],
                                                              scale=st_["rr"][:, 0:1]),
                    waits=[t_nb, xnslot_consumers[sl][-1] if xnslot_consumers[sl] else None])
        prev_stats[sl] = t_xn
        xnslot_consumers[sl] = []
        xslot_consumers[sl] = [t_sq, t_rs1, t_xn]
        for grp in range(4):
            bk = (j * 4 + grp) % 2
            t_tr = None
            for q in range(4):
                fc = grp * 4 + q
                t_tr = b.op(
                    "tensor",
                    lambda e, s, bk=bk, q=q, fc=fc, sl=sl: e.transpose(
                        pb16[bk][:, q * 128:(q + 1) * 128],
                        xn_tm[sl][:, fc * 128:(fc + 1) * 128], eye16[:]),
                    waits=([t_xn, t_eye16, tb16_free[bk]] if q == 0 else []),
                    notok=(q < 3))
            t_ev = b.op(
                "scalar",
                lambda e, s, bk=bk, grp=grp, j=j: e.copy(
                    xn[:, grp * 4:(grp + 1) * 4, j * 128:(j + 1) * 128],
                    pb16[bk][:, 0:512].rearrange("p (a c) -> p a c", a=4)),
                waits=[t_tr])
            tb16_free[bk] = t_ev
            xnslot_consumers[sl].append(t_ev)
            xn_t4_tok[j // 4] = t_ev

    b.barrier()
    p1.close()
    p2 = contextlib.ExitStack()

    def sb2(name, shape, dtype=F32):
        sbufs[name] = p2.enter_context(nc.sbuf_tensor(name, shape, dtype))
        return sbufs[name]

    stg_w = [sb2(f"stg_w{i}", [128, NE, 512], BF16) for i in range(2)]
    wcol = [sb2(f"wcol{i}", [128, NE, 128], BF16) for i in range(2)]
    winp = sb2("winp", [128, T])
    sgrow = sb2("sgrow", [128, T])
    irow = sb2("irow", [128, T])
    drow = [sb2(f"drow{i}", [128, T]) for i in range(2)]
    bprow = [sb2(f"bprow{i}", [128, T]) for i in range(2)]
    gaterow = [sb2(f"gaterow{i}", [128, T]) for i in range(2)]
    strash = sb2("strash", [128, T])
    prow = sb2("prow", [128, T])

    WC = [b.sem("WC0"), b.sem("WC1")]
    SD = [b.sem("SD0"), b.sem("SD1")]
    SBP = [b.sem("SBP0"), b.sem("SBP1")]
    SGA = [b.sem("SGA0"), b.sem("SGA1")]
    wcol_consumer = [None, None]
    row_consumers = {"winp": [], "sg": [], "irow": [], "strash": [], "prow": []}
    drow_cons = [[], []]
    bprow_cons = [[], []]
    gaterow_cons = [[], []]
    mmcnt = [0]

    wsrc = {0: w_in_h, 1: w_dec_h, 2: w_gate_h}
    use = 0
    for e_ in range(NE):
        emit_precast(2 if NE >= 16 else 32 // NE)
        evacs = {}
        for widx in range(3):
            slw = use % 2
            t_w = b.op(
                "gpsimd",
                lambda e, s, widx=widx, e_=e_, slw=slw: e.dma_start(
                    wcol[slw][:], wsrc[widx][e_, :, :, :]),
                waits=[wcol_consumer[slw]], dma=WC[slw])
            use += 1
            evs = []
            bkmap = list(range(NT4))
            mm_last = [None] * NT4
            for t4 in range(NT4):
                bk = bkmap[t4]
                for k in range(NE):
                    mm_last[t4] = b.op(
                        "tensor",
                        lambda e, s, bk=bk, k=k, t4=t4, slw=slw: e.matmul(
                            banks6[bk][:, 0:TT], wcol[slw][:, k, :],
                            xn[:, k, t4 * TT:(t4 + 1) * TT],
                            start=(k == 0), stop=(k == NE - 1)),
                        waits=([t_w, bank_free[bk]] if k == 0 else []),
                        notok=(k < NE - 1))
            for t4 in range(NT4):
                bk = bkmap[t4]
                t_mm = mm_last[t4]
                if widx == 0:
                    t_e1 = b.op("scalar",
                                lambda e, s, bk=bk, t4=t4, e_=e_: e.activation(
                                    winp[:, t4 * TT:(t4 + 1) * TT], banks6[bk][:, 0:TT],
                                    A.Identity, bias=bi_t[:, e_:e_ + 1]),
                                waits=[t_mm] + row_consumers["winp"])
                    t_e2 = b.op("scalar",
                                lambda e, s, bk=bk, t4=t4, e_=e_: e.activation(
                                    sgrow[:, t4 * TT:(t4 + 1) * TT], banks6[bk][:, 0:TT],
                                    A.Sigmoid, bias=bi_t[:, e_:e_ + 1]),
                                waits=row_consumers["sg"])
                    bank_free[bk] = t_e2
                    evs += [t_e1, t_e2]
                elif widx == 1:
                    sld = e_ % 2
                    t_e1 = b.op("scalar",
                                lambda e, s, bk=bk, t4=t4, e_=e_, sld=sld: e.activation(
                                    drow[sld][:, t4 * TT:(t4 + 1) * TT], banks6[bk][:, 0:TT],
                                    A.Sigmoid, bias=bdc_t[:, e_:e_ + 1]),
                                waits=[t_mm, t_bdc] + drow_cons[sld] +
                                      [("SD" + str(sld), b.total["SD" + str(sld)])])
                    if t4 == 0:
                        drow_cons[sld] = []
                    bank_free[bk] = t_e1
                    evs.append(t_e1)
                else:
                    sld = e_ % 2
                    t_e1 = b.op("scalar",
                                lambda e, s, bk=bk, t4=t4, e_=e_, sld=sld: e.activation(
                                    gaterow[sld][:, t4 * TT:(t4 + 1) * TT], banks6[bk][:, 0:TT],
                                    A.Sigmoid, bias=bg_t[:, e_:e_ + 1]),
                                waits=[t_mm] + gaterow_cons[sld] +
                                      [("SGA" + str(sld), b.total["SGA" + str(sld)])])
                    if t4 == 0:
                        gaterow_cons[sld] = []
                    bank_free[bk] = t_e1
                    evs.append(t_e1)
            wcol_consumer[slw] = mm_last[NT4 - 1]
            evacs[widx] = evs

        sld = e_ % 2
        t_i = b.op("vector", lambda e, s: e.tensor_mul(irow[:], winp[:], sgrow[:]),
                   waits=evacs[0] + row_consumers["irow"])
        row_consumers["winp"] = [t_i]
        row_consumers["sg"] = [t_i]
        t_bp = b.op("vector",
                    lambda e, s, sld=sld: e.scalar_tensor_tensor(
                        bprow[sld][:], drow[sld][:], 1.0, irow[:], ALU.subtract, ALU.mult),
                    waits=evacs[1] + [t_i] + bprow_cons[sld] +
                          [("SBP" + str(sld), b.total["SBP" + str(sld)])])
        bprow_cons[sld] = []
        row_consumers["irow"] = [t_bp]
        t_sc = b.op("vector",
                    lambda e, s, sld=sld: e.tensor_tensor_scan(
                        strash[:], drow[sld][:], bprow[sld][:], 0.0, ALU.mult, ALU.subtract),
                    waits=[t_bp] + row_consumers["strash"])
        t_cs = b.op("vector",
                    lambda e, s, e_=e_: e.tensor_copy(pair[:, NE + e_:NE + e_ + 1],
                                                      strash[:, T - 1:T]),
                    waits=[t_sc])
        row_consumers["strash"] = [t_cs]
        t_ps = b.op("vector",
                    lambda e, s, sld=sld: e.tensor_tensor_scan(
                        prow[:], drow[sld][:], drow[sld][:], 1.0, ALU.mult, ALU.bypass),
                    waits=[t_sc] + row_consumers["prow"])
        t_cp = b.op("vector",
                    lambda e, s, e_=e_: e.tensor_copy(pair[:, e_:e_ + 1], prow[:, T - 1:T]),
                    waits=[t_ps])
        row_consumers["prow"] = [t_cp]
        t_sd = b.op("sync", lambda e, s, sld=sld, e_=e_: e.dma_start(sc_d[e_, :, :], drow[sld][:]),
                    waits=[t_ps], dma=SD[sld])
        t_sb_ = b.op("sync", lambda e, s, sld=sld, e_=e_: e.dma_start(sc_bp[e_, :, :], bprow[sld][:]),
                     waits=[t_sc], dma=SBP[sld])
        t_sg_ = b.op("sync", lambda e, s, sld=sld, e_=e_: e.dma_start(sc_gate[e_, :, :], gaterow[sld][:]),
                     waits=evacs[2], dma=SGA[sld])
        drow_cons[sld] = [t_ps]
        bprow_cons[sld] = [t_sc]
        gaterow_cons[sld] = []

    CI = b.sem("CI")
    CCS = b.sem("CCS")
    GS = b.sem("GS")
    pair_done = ("VV", b.total["VV"])
    t_ci = b.op("sync", lambda e, s: e.dma_start(cc_in[:, :], pair[:]),
                waits=[pair_done], dma=CI)

    p2_end = b.snapshot()
    b.barrier(p2_end, exclude=("CI",))
    p2.close()

    p4a = contextlib.ExitStack()

    def sb4(name, shape, dtype=F32):
        sbufs[name] = p4a.enter_context(nc.sbuf_tensor(name, shape, dtype))
        return sbufs[name]

    wgu_g = [sb4(f"wgu_g{i}", [128, NE, 128], BF16) for i in range(2)]
    wgu_u = [sb4(f"wgu_u{i}", [128, NE, 128], BF16) for i in range(2)]
    sg_t = [sb4(f"sg_t{i}", [128, T]) for i in range(2)]
    h_row = [sb4(f"h_row{i}", [128, T], BF16) for i in range(2)]

    WG = [b.sem("WG0"), b.sem("WG1")]
    WU = [b.sem("WU0"), b.sem("WU1")]
    SH = [b.sem("SH0"), b.sem("SH1")]
    wg_consumer = [None, None]
    wu_consumer = [None, None]
    sg_consumer = [None, None]
    hrow_cons = [[], []]

    for i in range(NIC):
        sl = i % 2
        t_wg = b.op("gpsimd",
                    lambda e, s, i=i, sl=sl: e.dma_start(
                        wgu_g[sl][:], w_gu_h[i, :, :, :]),
                    waits=[wg_consumer[sl]], dma=WG[sl])
        t_wu = b.op("gpsimd",
                    lambda e, s, i=i, sl=sl: e.dma_start(
                        wgu_u[sl][:], w_gu_h[NIC + i, :, :, :]),
                    waits=[wu_consumer[sl]], dma=WU[sl])
        h_ops = []
        bkg = list(range(NT4))
        mm_last = [None] * NT4
        for t4 in range(NT4):
            bk = bkg[t4]
            for k in range(NE):
                mm_last[t4] = b.op("tensor",
                                   lambda e, s, bk=bk, k=k, t4=t4, sl=sl: e.matmul(
                                       banks6[bk][:, 0:TT], wgu_g[sl][:, k, :],
                                       xn[:, k, t4 * TT:(t4 + 1) * TT],
                                       start=(k == 0), stop=(k == NE - 1)),
                                   waits=([t_wg, bank_free[bk]] if k == 0 else []),
                                   notok=(k < NE - 1))
        wg_consumer[sl] = mm_last[NT4 - 1]
        sg_evs = []
        for t4 in range(NT4):
            bk = bkg[t4]
            t_sg2 = b.op("scalar",
                         lambda e, s, bk=bk, t4=t4, sl=sl: e.activation(
                             sg_t[sl][:, t4 * TT:(t4 + 1) * TT], banks6[bk][:, 0:TT], A.Silu),
                         waits=[mm_last[t4], sg_consumer[sl]])
            sg_consumer[sl] = None
            bank_free[bk] = t_sg2
            sg_evs.append(t_sg2)
        bku = [4, 5, 0, 1][:NT4]
        mm_lastu = [None] * NT4
        for t4 in range(NT4):
            bk = bku[t4]
            for k in range(NE):
                mm_lastu[t4] = b.op("tensor",
                                    lambda e, s, bk=bk, k=k, t4=t4, sl=sl: e.matmul(
                                        banks6[bk][:, 0:TT], wgu_u[sl][:, k, :],
                                        xn[:, k, t4 * TT:(t4 + 1) * TT],
                                        start=(k == 0), stop=(k == NE - 1)),
                                    waits=([t_wu, bank_free[bk]] if k == 0 else []),
                                    notok=(k < NE - 1))
        wu_consumer[sl] = mm_lastu[NT4 - 1]
        for t4 in range(NT4):
            bk = bku[t4]
            t_h = b.op("vector",
                       lambda e, s, bk=bk, t4=t4, sl=sl: e.tensor_mul(
                           h_row[sl][:, t4 * TT:(t4 + 1) * TT],
                           sg_t[sl][:, t4 * TT:(t4 + 1) * TT], banks6[bk][:, 0:TT]),
                       waits=[mm_lastu[t4], sg_evs[t4]] + (hrow_cons[sl] if t4 == 0 else []))
            if t4 == 0:
                hrow_cons[sl] = []
            bank_free[bk] = t_h
            h_ops.append(t_h)
        sg_consumer[sl] = h_ops[-1]
        t_sh = b.op("sync", lambda e, s, i=i, sl=sl: e.dma_start(sc_h[i, :, :], h_row[sl][:]),
                    waits=h_ops + [("SH" + str(sl), b.total["SH" + str(sl)])], dma=SH[sl])
        hrow_cons[sl] = [t_sh]

    p4a_end = b.snapshot()
    mm_p4a = ("MM", b.total["MM"])
    vv_p4a = ("VV", b.total["VV"])
    sc_p4a = ("SC", b.total["SC"])
    sh_tots = [("SH0", b.total["SH0"]), ("SH1", b.total["SH1"])]
    p4a_all = [mm_p4a, vv_p4a, sc_p4a] + sh_tots
    p4a.close()
    xn_stack.close()

    p4b = contextlib.ExitStack()

    def sb4b(name, shape, dtype=F32):
        sbufs[name] = p4b.enter_context(nc.sbuf_tensor(name, shape, dtype))
        return sbufs[name]

    wdn_q = [sb4b(f"wdn_q{i}", [128, NIC, 512], BF16) for i in range(2)]
    hh = [sb4b(f"hh{i}", [128, NIC, TQ], BF16) for i in range(2)]
    g_t = [sb4b(f"g_t{i}", [128, 4, TQ]) for i in range(2)]

    WD = [b.sem("WD0"), b.sem("WD1")]
    HH = [b.sem("HH0"), b.sem("HH1")]
    SGSP = [b.sem("SGSP0"), b.sem("SGSP1")]
    hh_consumer = [[], []]
    gt_consumer = [[], []]

    WDg = [b.sem("WDg0"), b.sem("WDg1")]

    def load_wdn(q, waits, eng="sync"):
        sem = WDg[q % 2] if eng == "gpsimd" else WD[q % 2]
        t_wd = None
        for part in range(4):
            t_wd = b.op(eng,
                        lambda e, s, q=q, part=part: e.dma_start(
                            wdn_q[q % 2][:, part * 16:(part + 1) * 16, :],
                            sc_wdn[q, part, :, :, :]),
                        waits=waits if part == 0 else [], dma=sem)
        return t_wd

    def load_hh(t8, waits):
        sl = t8 % 2
        return b.op("sync",
                    lambda e, s, t8=t8, sl=sl: e.dma_start(
                        hh[sl][:],
                        sc_h.rearrange("i p t -> p i t")[:, :, t8 * TQ:(t8 + 1) * TQ]),
                    waits=waits, dma=HH[sl])

    if T == 2048:
        hh_pre = {0: load_hh(0, sh_tots)}
        if NT8 > 1:
            hh_pre[1] = load_hh(1, [])
        wd_tok = {0: load_wdn(0, [mm_p4a], eng="scalar")}
        wd_tok[1] = load_wdn(1, [vv_p4a] + sh_tots, eng="scalar")
    else:
        hh_pre = {0: load_hh(0, p4a_all)}
        if NT8 > 1:
            hh_pre[1] = load_hh(1, [])
        wd_tok = {0: load_wdn(0, p4a_all, eng="scalar")}
        wd_tok[1] = load_wdn(1, [], eng="scalar")

    b.barrier(p4a_end, exclude=("CI",))

    t_cc = b.op("gpsimd",
                lambda e, s: e.collective_compute(
                    "AllGather", ALU.bypass,
                    replica_groups=[list(range(NCORES))],
                    ins=[cc_in[:, :]], outs=[cc_out[:, :, :]]),
                waits=[t_ci], inc=("CCS", 1))
    t_gs = b.op("gpsimd",
                lambda e, s: e.dma_start(gsb[:], cc_out.rearrange("r p c -> p r c")),
                waits=[t_cc], dma=GS)

    q2_mm = [None]
    for q in range(4):
        last_mm = None
        for t8 in range(NT8):
            sl = t8 % 2
            if q == 0 and t8 in hh_pre:
                t_hh = hh_pre[t8]
            else:
                t_hh = load_hh(t8, hh_consumer[sl])
            hh_consumer[sl] = []
            gops = []
            for hc in range(4):
                bk = mmcnt[0] % 4
                mmcnt[0] += 1
                t_mm = None
                for ic in range(NIC):
                    t_mm = b.op("tensor",
                                lambda e, s, bk=bk, ic=ic, hc=hc, sl=sl, q=q: e.matmul(
                                    pb[bk][:, 0:TQ], wdn_q[q % 2][:, ic, hc * 128:(hc + 1) * 128],
                                    hh[sl][:, ic, :],
                                    start=(ic == 0), stop=(ic == NIC - 1)),
                                waits=([wd_tok[q], t_hh, bank_free[bk]] if ic == 0 else []),
                                notok=(ic < NIC - 1))
                last_mm = t_mm
                t_gt = b.op("scalar",
                            lambda e, s, bk=bk, hc=hc, sl=sl: e.activation(
                                g_t[sl][:, hc, :], pb[bk][:, 0:TQ], A.Silu),
                            waits=[t_mm] + (gt_consumer[sl] if hc == 0 else []))
                if hc == 0:
                    gt_consumer[sl] = []
                bank_free[bk] = t_gt
                gops.append(t_gt)
                hh_consumer[sl].append(t_mm)
            t_sg3 = b.op("scalar",
                         lambda e, s, q=q, t8=t8, sl=sl: e.dma_start(
                             sc_g.rearrange("f p t -> p f t")[:, q * 4:(q + 1) * 4,
                                                              t8 * TQ:(t8 + 1) * TQ],
                             g_t[sl][:]),
                         waits=gops + [("SGSP" + str(sl), b.total["SGSP" + str(sl)])],
                         dma=SGSP[sl])
            gt_consumer[sl] = [t_sg3]
        if q + 2 <= 3:
            wd_tok[q + 2] = load_wdn(q + 2, [last_mm], eng="gpsimd")
        if q == 2:
            q2_mm[0] = last_mm

    p4b_end = b.snapshot()
    p4b.close()

    p5 = contextlib.ExitStack()

    def sb5(name, shape, dtype=F32):
        sbufs[name] = p5.enter_context(nc.sbuf_tensor(name, shape, dtype))
        return sbufs[name]

    wout = sb5("wout", [128, NE, D], BF16)
    WO2 = b.sem("WO2")
    wo_all = b.op("sync",
                  lambda e, s: e.dma_start(wout[:], sc_wout.rearrange("k p c -> p k c")),
                  waits=[q2_mm[0]], dma=WO2)

    b.barrier(p4b_end)

    d5 = sb5("d5", [128, NE, TQ])
    bp5 = sb5("bp5", [128, NE, TQ])
    ga5 = sb5("ga5", [128, NE, TQ])
    gg5 = sb5("gg5", [128, NE, TQ])
    st5 = sb5("st5", [128, NE, TQ])
    y5 = sb5("y5", [128, NE, TQ], BF16)
    s5 = sb5("s5", [128, NE, TQ])
    x5 = [sb5(f"x5_{i}", [128, D]) for i in range(2)]
    out5 = [sb5(f"out5{i}", [128, D]) for i in range(2)]

    gs_tok = ("GS", b.total["GS"])
    t_h0 = b.op("vector", lambda e, s: e.memset(hcomb[:], 0.0), waits=[gs_tok, l0_tok])
    prev = t_h0
    for jcore in range(NCORES - 1):
        t_a = b.op("vector",
                   lambda e, s, j=jcore: e.tensor_mul(tmp1[:], gsb[:, j, 0:NE], hcomb[:]),
                   waits=[prev])
        t_b2 = b.op("vector",
                    lambda e, s, j=jcore: e.tensor_add(tmp1[:], tmp1[:], gsb[:, j, NE:2 * NE]),
                    waits=[t_a])
        t_c = b.op("vector", lambda e, s: e.tensor_sub(tmp2[:], tmp1[:], hcomb[:]),
                   waits=[t_b2])
        t_d2 = b.op("vector",
                    lambda e, s, j=jcore: e.tensor_scalar_mul(tmp2[:], tmp2[:],
                                                              cmask[:, j:j + 1]),
                    waits=[t_c])
        prev = b.op("vector", lambda e, s: e.tensor_add(hcomb[:], hcomb[:], tmp2[:]),
                    waits=[t_d2])
    t_carry0 = b.op("vector", lambda e, s: e.tensor_copy(carry[:], hcomb[:]), waits=[prev])

    LD5 = b.sem("LD5")
    LB5 = b.sem("LB5")
    LG5 = b.sem("LG5")
    LGG = b.sem("LGG")
    LX5 = [b.sem("LX5a"), b.sem("LX5b")]
    SOUT = [b.sem("SOUT0"), b.sem("SOUT1")]
    y5_cons = []
    out5_cons = [[], []]
    carry_last = [t_carry0]

    def ld_sc(eng, sem, dst, src, t8, waits):
        return b.op(eng,
                    lambda e, s, t8=t8, dst=dst, src=src: e.dma_start(
                        dst[:], src.rearrange("f p t -> p f t")[:, :, t8 * TQ:(t8 + 1) * TQ]),
                    waits=waits, dma=sem)

    t_ld = ld_sc("sync", LD5, d5, sc_d, 0, [])
    t_lb = ld_sc("sync", LB5, bp5, sc_bp, 0, [])
    t_lg = ld_sc("gpsimd", LG5, ga5, sc_gate, 0, [])
    t_lgg = ld_sc("sync", LGG, gg5, sc_g, 0, [])

    t_x5 = [b.op("gpsimd",
                 lambda e, s, j2=j2: e.dma_start(x5[j2][:], x_h[j2 * 128:(j2 + 1) * 128, :]),
                 dma=LX5[j2]) for j2 in range(2)]

    def emit_scans(t8, t_ld, t_lb, first_extra=()):
        sc = []
        prev = carry_last[0]
        for fc in range(NE):
            t_s5 = b.op("vector",
                        lambda e, s, fc=fc, t8=t8: e.tensor_tensor_scan(
                            st5[:, fc, :], d5[:, fc, :], bp5[:, fc, :],
                            carry[:, fc:fc + 1], ALU.mult, ALU.subtract),
                        waits=([t_ld, t_lb, prev] + list(first_extra)) if fc == 0
                              else [sc[fc - 1]])
            sc.append(t_s5)
        t_cy = b.op("vector",
                    lambda e, s: e.tensor_copy(
                        carry[:, 0:NE], st5[:, :, TQ - 1:TQ].rearrange("p f o -> p (f o)")),
                    waits=[sc[-1]])
        carry_last[0] = t_cy
        return sc

    scans = emit_scans(0, t_ld, t_lb)
    pend = None
    for t8 in range(NT8):
        t_y = b.op("vector", lambda e, s: e.tensor_mul(y5[:], ga5[:], st5[:]),
                   waits=[t_lg] + scans + y5_cons)
        y5_cons = []
        if t8 + 1 < NT8:
            t_ldn = ld_sc("sync", LD5, d5, sc_d, t8 + 1, [scans[-1]])
            t_lbn = ld_sc("sync", LB5, bp5, sc_bp, t8 + 1, [scans[-1]])
            t_lgn = ld_sc("gpsimd", LG5, ga5, sc_gate, t8 + 1, [t_y])
            scans = emit_scans(t8 + 1, t_ldn, t_lbn, first_extra=[t_y])
        if pend is not None:
            pt8, pcps, px5 = pend
            adds = []
            for j2 in range(TQ // 128):
                row0 = pt8 * TQ + j2 * 128
                t_ad = b.op("vector",
                            lambda e, s, j2=j2: e.tensor_add(
                                out5[j2][:], out5[j2][:], x5[j2][:]),
                            waits=pcps[j2] + [px5[j2]])
                adds.append(t_ad)
                t_st = b.op("sync",
                            lambda e, s, row0=row0, j2=j2: e.dma_start(
                                out_h[row0:row0 + 128, :], out5[j2][:]),
                            waits=[t_ad], dma=SOUT[j2])
                out5_cons[j2] = [t_st]
            t_x5 = [b.op("gpsimd",
                         lambda e, s, j2=j2, t8=t8: e.dma_start(
                             x5[j2][:], x_h[t8 * TQ + j2 * 128:t8 * TQ + (j2 + 1) * 128, :]),
                         waits=[adds[j2]], dma=LX5[j2])
                    for j2 in range(2)]
        ev5 = []
        for e_ in range(NE):
            bk = e_ % 2
            t_mm = None
            for k in range(NE):
                t_mm = b.op("tensor",
                            lambda e, s, bk=bk, k=k, e_=e_: e.matmul(
                                pb[bk][:, 0:TQ], wout[:, k, e_ * 128:(e_ + 1) * 128],
                                y5[:, k, :], start=(k == 0), stop=(k == NE - 1)),
                            waits=([wo_all, t_y, bank_free[bk]]
                                   if k == 0 else []),
                            notok=(k < NE - 1))
            y5_cons = [t_mm]
            t_e5 = b.op("scalar",
                        lambda e, s, bk=bk, e_=e_: e.activation(
                            s5[:, e_, :], pb[bk][:, 0:TQ], A.Identity,
                            bias=bo_t[:, e_:e_ + 1]),
                        waits=[t_mm])
            bank_free[bk] = t_e5
            ev5.append(t_e5)
        t_rg = []
        for g4 in range(4):
            t_rg.append(b.op("vector",
                             lambda e, s, g4=g4: e.tensor_mul(
                                 s5[:, g4 * 4:(g4 + 1) * 4, :],
                                 s5[:, g4 * 4:(g4 + 1) * 4, :],
                                 gg5[:, g4 * 4:(g4 + 1) * 4, :]),
                             waits=[ev5[g4 * 4 + 3], t_lgg] if g4 == 0
                                   else [ev5[g4 * 4 + 3]]))
        if t8 + 1 < NT8:
            t_lggn = ld_sc("sync", LGG, gg5, sc_g, t8 + 1, [t_rg[-1]])
        cps = {0: [], 1: []}
        for j2 in range(TQ // 128):
            for grp in range(4):
                bk = (j2 * 4 + grp) % 2
                t_tr = None
                for qq in range(4):
                    fc = grp * 4 + qq
                    t_tr = b.op("tensor",
                                lambda e, s, bk=bk, qq=qq, fc=fc, j2=j2: e.transpose(
                                    banks6[4 + bk][:, qq * 128:(qq + 1) * 128],
                                    s5[:, fc, j2 * 128:(j2 + 1) * 128], eye32[:]),
                                waits=([t_rg[grp], bank_free[4 + bk], l0_tok]
                                       if qq == 0 else []),
                                notok=(qq < 3))
                t_cp = b.op("scalar",
                            lambda e, s, bk=bk, grp=grp, j2=j2: e.copy(
                                out5[j2][:, grp * 512:(grp + 1) * 512],
                                banks6[4 + bk][:, 0:512]),
                            waits=[t_tr] + (out5_cons[j2] if grp == 0 else []))
                if grp == 0:
                    out5_cons[j2] = []
                bank_free[4 + bk] = t_cp
                cps[j2].append(t_cp)
        pend = (t8, cps, t_x5)
        if t8 + 1 < NT8:
            t_ld, t_lb, t_lg, t_lgg = t_ldn, t_lbn, t_lgn, t_lggn

    pt8, pcps, px5 = pend
    for j2 in range(TQ // 128):
        row0 = pt8 * TQ + j2 * 128
        t_ad = b.op("vector",
                    lambda e, s, j2=j2: e.tensor_add(out5[j2][:], out5[j2][:], x5[j2][:]),
                    waits=pcps[j2] + [px5[j2]])
        t_st = b.op("sync",
                    lambda e, s, row0=row0, j2=j2: e.dma_start(
                        out_h[row0:row0 + 128, :], out5[j2][:]),
                    waits=[t_ad], dma=SOUT[j2])
        out5_cons[j2] = [t_st]

    b.wait("sync", [("SOUT0", b.total["SOUT0"]), ("SOUT1", b.total["SOUT1"])])

    sem_handles = {}
    for name in b.sem_names:
        sem_handles[name] = es.enter_context(nc.semaphore(name))
    with nc.Block() as blk:
        @blk.tensor
        def _(e):
            b.replay("tensor", e, sem_handles)

        @blk.scalar
        def _(e):
            b.replay("scalar", e, sem_handles)

        @blk.vector
        def _(e):
            b.replay("vector", e, sem_handles)

        @blk.gpsimd
        def _(e):
            b.replay("gpsimd", e, sem_handles)

        @blk.sync
        def _(e):
            b.replay("sync", e, sem_handles)
    p5.close()
    es.close()
    return nc


_NC_CACHE = {}


def _get_nc(T):
    if T not in _NC_CACHE:
        _NC_CACHE[T] = build_nc(T)
    return _NC_CACHE[T]


def _bias_tab(v):
    return np.ascontiguousarray(np.asarray(v, dtype=np.float32).reshape(NE, 128).T)


def _attn_blocks(w):
    w = np.asarray(w, dtype=np.float32)
    return np.ascontiguousarray(w.reshape(NE, 128, NE, 128).transpose(2, 1, 0, 3))


def kernel(L0_attn_x, L0_attn_W_in, L0_attn_b_in, L0_attn_W_decay, L0_attn_b_decay,
           L0_attn_W_gate, L0_attn_b_gate, L0_attn_W_out, L0_attn_b_out,
           L0_attn_decay_bias, L0_ffn_w_gate_up, L0_ffn_w_down):
    T = SEQ // NCORES
    x = np.ascontiguousarray(np.asarray(L0_attn_x, dtype=np.float32).reshape(SEQ, D))
    w_gu = np.asarray(L0_ffn_w_gate_up, dtype=np.float32)
    w_gu_r = np.ascontiguousarray(
        w_gu.reshape(2 * NIC, 128, NE, 128).transpose(0, 3, 2, 1))
    w_dn = np.asarray(L0_ffn_w_down, dtype=np.float32)
    w_dn_r = np.ascontiguousarray(
        w_dn.T.reshape(4, 16, 128, 4, 512).transpose(3, 0, 2, 1, 4))
    w_out = np.ascontiguousarray(np.asarray(L0_attn_W_out, dtype=np.float32))
    base = dict(
        w_in_r=_attn_blocks(L0_attn_W_in), w_dec_r=_attn_blocks(L0_attn_W_decay),
        w_gate_r=_attn_blocks(L0_attn_W_gate), w_out=w_out,
        w_gu_r=w_gu_r, w_dn_r=w_dn_r,
        b_in_t=_bias_tab(L0_attn_b_in), b_dec_t=_bias_tab(L0_attn_b_decay),
        b_db_t=_bias_tab(L0_attn_decay_bias), b_gate_t=_bias_tab(L0_attn_b_gate),
        b_out_t=_bias_tab(L0_attn_b_out),
        eye32=np.eye(128, dtype=np.float32),
    )
    in_maps = []
    for c in range(NCORES):
        m = dict(base)
        m["x"] = np.ascontiguousarray(x[c * T:(c + 1) * T, :])
        cm = np.zeros((128, NCORES), dtype=np.float32)
        cm[:, :c] = 1.0
        m["cmask"] = cm
        in_maps.append(m)
    nc = _get_nc(T)
    res = run_bass_kernel_spmd(nc, in_maps, core_ids=list(range(NCORES)))
    out = np.concatenate([np.asarray(r["out"]) for r in res.results], axis=0)
    return out.reshape(1, SEQ, D).astype(np.float32)


# version 13
# speedup vs baseline: 1.0054x; 1.0054x over previous
import os
os.environ.setdefault("BASS_INTERP_NUM_WORKERS", "8")

import contextlib
import numpy as np

import concourse.bass as bass
import concourse.mybir as mybir
from concourse.bass_utils import run_bass_kernel_spmd

F32 = mybir.dt.float32
BF16 = mybir.dt.bfloat16
A = mybir.ActivationFunctionType
ALU = mybir.AluOpType
AX = mybir.AxisListType

D = 2048
NE = D // 128
INTER = 8192
NIC = INTER // 128
SEQ = 16384
NCORES = 8
EPS = 1e-5


class KB:

    ENGS = ("tensor", "scalar", "vector", "gpsimd", "sync")

    def __init__(self):
        self.recs = {e: [] for e in self.ENGS}
        self.sem_names = []
        self.total = {}
        self.selfsem = {"tensor": "MM", "scalar": "SC", "vector": "VV"}
        for s in ("MM", "SC", "VV"):
            self._reg(s)

    def _reg(self, name):
        if name not in self.total:
            self.sem_names.append(name)
            self.total[name] = 0

    def sem(self, name):
        self._reg(name)
        return name

    def op(self, eng, fn, waits=(), dma=None, inc=None, notok=False):
        waits = [w for w in waits if w is not None]
        if notok:
            self.recs[eng].append((fn, waits, None))
            return None
        if dma is not None:
            self._reg(dma)
            self.total[dma] += 16
            tok = (dma, self.total[dma])
            self.recs[eng].append((fn, waits, (dma, 16)))
            return tok
        if inc is not None:
            s, n = inc
            self._reg(s)
            self.total[s] += n
            tok = (s, self.total[s])
            self.recs[eng].append((fn, waits, (s, n)))
            return tok
        if eng in self.selfsem:
            s = self.selfsem[eng]
            self.total[s] += 1
            tok = (s, self.total[s])
            self.recs[eng].append((fn, waits, (s, 1)))
            return tok
        self.recs[eng].append((fn, waits, None))
        return None

    def wait(self, eng, toks):
        toks = [t for t in toks if t is not None]
        if toks:
            self.recs[eng].append((None, toks, None))

    def snapshot(self):
        return dict(self.total)

    def barrier(self, snap=None, exclude=()):
        snap = snap or self.snapshot()
        toks = [(s, v) for s, v in snap.items() if v > 0 and s not in exclude]
        for e in self.ENGS:
            self.wait(e, toks)

    def replay(self, eng_name, eng, sems):
        waited = {}
        for fn, waits, inc in self.recs[eng_name]:
            for (s, v) in waits:
                if v > 0 and waited.get(s, -1) < v:
                    eng.wait_ge(sems[s], v)
                    waited[s] = v
            if fn is not None:
                instr = fn(eng, sems)
                if inc is not None and instr is not None:
                    instr.then_inc(sems[inc[0]], inc[1])


def build_nc(T):
    NJ = T // 128
    TT = min(512, T)
    NT4 = T // TT
    TQ = min(256, T)
    NT8 = T // TQ

    nc = bass.Bass("TRN2", target_bir_lowering=False)

    x_h = nc.dram_tensor("x", [T, D], F32, kind="ExternalInput").ap()
    w_in_h = nc.dram_tensor("w_in_r", [NE, 128, NE, 128], F32, kind="ExternalInput").ap()
    w_dec_h = nc.dram_tensor("w_dec_r", [NE, 128, NE, 128], F32, kind="ExternalInput").ap()
    w_gate_h = nc.dram_tensor("w_gate_r", [NE, 128, NE, 128], F32, kind="ExternalInput").ap()
    w_out_h = nc.dram_tensor("w_out", [D, D], F32, kind="ExternalInput").ap()
    w_gu_h = nc.dram_tensor("w_gu_r", [2 * NIC, 128, NE, 128], F32, kind="ExternalInput").ap()
    w_dn_h = nc.dram_tensor("w_dn_r", [4, 4, 128, 16, 512], F32, kind="ExternalInput").ap()
    b_in_h = nc.dram_tensor("b_in_t", [128, NE], F32, kind="ExternalInput").ap()
    b_dec_h = nc.dram_tensor("b_dec_t", [128, NE], F32, kind="ExternalInput").ap()
    b_db_h = nc.dram_tensor("b_db_t", [128, NE], F32, kind="ExternalInput").ap()
    b_gate_h = nc.dram_tensor("b_gate_t", [128, NE], F32, kind="ExternalInput").ap()
    b_out_h = nc.dram_tensor("b_out_t", [128, NE], F32, kind="ExternalInput").ap()
    eye_h = nc.dram_tensor("eye32", [128, 128], F32, kind="ExternalInput").ap()
    mask_h = nc.dram_tensor("cmask", [128, NCORES], F32, kind="ExternalInput").ap()
    out_h = nc.dram_tensor("out", [T, D], F32, kind="ExternalOutput").ap()

    sc_d = nc.dram_tensor("sc_d", [NE, 128, T], F32, kind="Internal").ap()
    sc_bp = nc.dram_tensor("sc_bp", [NE, 128, T], F32, kind="Internal").ap()
    sc_gate = nc.dram_tensor("sc_gate", [NE, 128, T], F32, kind="Internal").ap()
    sc_g = nc.dram_tensor("sc_g", [NE, 128, T], F32, kind="Internal").ap()
    sc_h = nc.dram_tensor("sc_h", [NIC, 128, T], BF16, kind="Internal").ap()
    cc_in = nc.dram_tensor("cc_in", [128, 2 * NE], F32, kind="Internal").ap()
    cc_out = nc.dram_tensor("cc_out", [NCORES, 128, 2 * NE], F32, kind="Internal").ap()
    sc_wdn = nc.dram_tensor("sc_wdn", [4, 4, 128, 16, 512], BF16, kind="Internal").ap()
    sc_wout = nc.dram_tensor("sc_wout", [NE, 128, D], BF16, kind="Internal").ap()

    b = KB()
    es = contextlib.ExitStack()
    sbufs = {}

    def sb(name, shape, dtype=F32):
        sbufs[name] = es.enter_context(nc.sbuf_tensor(name, shape, dtype))
        return sbufs[name]

    bi_t = sb("bi_t", [128, NE])
    bdc_t = sb("bdc_t", [128, NE])
    bdb_t = sb("bdb_t", [128, NE])
    bg_t = sb("bg_t", [128, NE])
    bo_t = sb("bo_t", [128, NE])
    eye32 = sb("eye32s", [128, 128])
    eye16 = sb("eye16s", [128, 128], BF16)
    cmask = sb("cmasks", [128, NCORES])
    pair = sb("pair", [128, 2 * NE])
    gsb = sb("gsb", [128, NCORES, 2 * NE])
    hcomb = sb("hcomb", [128, NE])
    tmp1 = sb("tmp1", [128, NE])
    tmp2 = sb("tmp2", [128, NE])
    carry = sb("carry", [128, NE])

    pb = [es.enter_context(nc.psum_tensor(f"pb{i}", [128, 512], F32)) for i in range(4)]
    pb16 = [es.enter_context(nc.psum_tensor(f"pb16_{i}", [128, 1024], BF16)) for i in range(2)]
    pb32 = [es.enter_context(nc.psum_tensor(f"pb32_{i}", [128, 512], F32)) for i in range(2)]
    banks6 = pb + pb32
    bank_free = [None] * 6
    tb16_free = [None] * 2
    ring = [0]

    def ring_next():
        bk = ring[0] % 6
        ring[0] += 1
        return bk

    xn_stack = contextlib.ExitStack()
    xn = xn_stack.enter_context(nc.sbuf_tensor("xn", [128, NE, T], BF16))

    L0 = b.sem("L0")
    for dst, src in ((bi_t, b_in_h), (bdc_t, b_dec_h), (bdb_t, b_db_h),
                     (bg_t, b_gate_h), (bo_t, b_out_h), (eye32, eye_h),
                     (cmask, mask_h)):
        b.op("sync", lambda e, s, dst=dst, src=src: e.dma_start(dst[:], src[:, :]),
             dma="L0")
    l0_tok = ("L0", b.total["L0"])
    t_eye16 = b.op("scalar", lambda e, s: e.copy(eye16[:], eye32[:]), waits=[l0_tok])
    t_bdc = b.op("vector", lambda e, s: e.tensor_add(bdc_t[:], bdc_t[:], bdb_t[:]),
                 waits=[l0_tok])

    p1 = contextlib.ExitStack()

    def sb1(name, shape, dtype=F32):
        sbufs[name] = p1.enter_context(nc.sbuf_tensor(name, shape, dtype))
        return sbufs[name]

    NSL = 4
    x_t = [sb1(f"x_t{i}", [128, D]) for i in range(NSL)]
    xn_tm = [sb1(f"xn_tm{i}", [128, D], BF16) for i in range(NSL)]
    st2 = [{k: sb1(f"st{i}_" + k, [128, 1]) for k in
            ("sx", "sq", "mean", "msq", "m2", "vv1", "vp", "s1", "r0",
             "aa", "bb", "cc", "rr", "nb")} for i in range(NSL)]
    xsqs = [sb1(f"xsqs{i}", [128, D]) for i in range(NSL)]

    STGL = [b.sem("STGL0"), b.sem("STGL1")]
    STGS = [b.sem("STGS0"), b.sem("STGS1")]
    precast_jobs = []
    for q in range(4):
        for part in range(4):
            precast_jobs.append((
                lambda sl, q=q, part=part: (stg_w[sl][:, :, :], w_dn_h[q, part, :, :, :]),
                lambda sl, q=q, part=part: (sc_wdn[q, part, :, :, :], stg_w[sl][:, :, :])))
    for k in range(NE):
        precast_jobs.append((
            lambda sl, k=k: (stg_w[sl][:, 0:4, :].rearrange("p a c -> p (a c)"),
                             w_out_h[k * 128:(k + 1) * 128, :]),
            lambda sl, k=k: (sc_wout[k, :, :],
                             stg_w[sl][:, 0:4, :].rearrange("p a c -> p (a c)"))))
    pc_i = [0]
    pc_store_tok = [None, None]
    xn_t4_tok = {}

    def emit_precast(njobs):
        for _ in range(njobs):
            i = pc_i[0]
            if i >= len(precast_jobs): return
            pc_i[0] += 1
            sl = i % 2
            ldfn, stfn = precast_jobs[i]
            t_l = b.op("gpsimd",
                       lambda e, s, ldfn=ldfn, sl=sl: e.dma_start(*ldfn(sl)),
                       waits=[pc_store_tok[sl]], dma=STGL[sl])
            pc_store_tok[sl] = b.op("sync",
                                    lambda e, s, stfn=stfn, sl=sl: e.dma_start(*stfn(sl)),
                                    waits=[t_l], dma=STGS[sl])

    LX = [b.sem(f"LX{i}") for i in range(NSL)]
    xslot_consumers = [[] for _ in range(NSL)]
    xnslot_consumers = [[] for _ in range(NSL)]
    xsq_consumer = [None] * NSL
    prev_stats = [None] * NSL

    for j in range(NJ):
        sl = j % NSL
        t_load = b.op(
            "sync",
            lambda e, s, sl=sl, j=j: e.dma_start(x_t[sl][:], x_h[j * 128:(j + 1) * 128, :]),
            waits=xslot_consumers[sl], dma=LX[sl])
        xslot_consumers[sl] = []

        st_ = st2[sl]
        myxsq = xsqs[sl]
        t_rs1 = b.op("scalar",
                     lambda e, s, sl=sl, myxsq=myxsq, st_=st_: e.activation(
                         myxsq[:], x_t[sl][:], A.Identity, accum_out=st_["sx"][:]),
                     waits=[t_load, prev_stats[sl], xsq_consumer[sl]])
        t_rs2 = b.op("scalar",
                     lambda e, s, sl=sl, myxsq=myxsq, st_=st_: e.activation(
                         myxsq[:], x_t[sl][:], A.Square, accum_out=st_["sq"][:]),
                     waits=[t_rs1])
        t_sq = t_rs2
        xsq_consumer[sl] = t_rs2
        inv = 1.0 / D
        t_m = b.op("vector", lambda e, s, st_=st_: e.tensor_scalar_mul(st_["mean"][:], st_["sx"][:], inv),
                   waits=[t_rs1])
        t_mq = b.op("vector", lambda e, s, st_=st_: e.tensor_scalar_mul(st_["msq"][:], st_["sq"][:], inv),
                    waits=[t_rs2])
        t_m2 = b.op("vector", lambda e, s, st_=st_: e.tensor_mul(st_["m2"][:], st_["mean"][:], st_["mean"][:]),
                    waits=[t_m])
        t_v1 = b.op("vector", lambda e, s, st_=st_: e.tensor_sub(st_["vv1"][:], st_["msq"][:], st_["m2"][:]),
                    waits=[t_mq, t_m2])
        t_vp = b.op("vector", lambda e, s, st_=st_: e.tensor_scalar_add(st_["vp"][:], st_["vv1"][:], EPS),
                    waits=[t_v1])
        t_s1 = b.op("scalar", lambda e, s, st_=st_: e.activation(st_["s1"][:], st_["vp"][:], A.Sqrt),
                    waits=[t_vp])
        t_r0 = b.op("vector", lambda e, s, st_=st_: e.reciprocal(st_["r0"][:], st_["s1"][:]),
                    waits=[t_s1])
        t_aa = b.op("vector", lambda e, s, st_=st_: e.tensor_mul(st_["aa"][:], st_["r0"][:], st_["r0"][:]),
                    waits=[t_r0])
        t_bb = b.op("vector", lambda e, s, st_=st_: e.tensor_mul(st_["bb"][:], st_["aa"][:], st_["vp"][:]),
                    waits=[t_aa, t_vp])
        t_cc = b.op("vector", lambda e, s, st_=st_: e.tensor_scalar(st_["cc"][:], st_["bb"][:], -0.5, 1.5,
                                                           ALU.mult, ALU.add),
                    waits=[t_bb])
        t_rr = b.op("vector", lambda e, s, st_=st_: e.tensor_mul(st_["rr"][:], st_["r0"][:], st_["cc"][:]),
                    waits=[t_cc])
        t_nb = b.op("vector", lambda e, s, st_=st_: e.tensor_scalar(st_["nb"][:], st_["mean"][:],
                                                           st_["rr"][:, 0:1], -1.0,
                                                           ALU.mult, ALU.mult),
                    waits=[t_rr])
        t_xn = b.op("scalar",
                    lambda e, s, sl=sl, st_=st_: e.activation(xn_tm[sl][:], x_t[sl][:], A.Identity,
                                                              bias=st_["nb"][:, 0:1],
                                                              scale=st_["rr"][:, 0:1]),
                    waits=[t_nb, xnslot_consumers[sl][-1] if xnslot_consumers[sl] else None])
        prev_stats[sl] = t_xn
        xnslot_consumers[sl] = []
        xslot_consumers[sl] = [t_sq, t_rs1, t_xn]
        for grp in range(4):
            bk = (j * 4 + grp) % 2
            t_tr = None
            for q in range(4):
                fc = grp * 4 + q
                t_tr = b.op(
                    "tensor",
                    lambda e, s, bk=bk, q=q, fc=fc, sl=sl: e.transpose(
                        pb16[bk][:, q * 128:(q + 1) * 128],
                        xn_tm[sl][:, fc * 128:(fc + 1) * 128], eye16[:]),
                    waits=([t_xn, t_eye16, tb16_free[bk]] if q == 0 else []),
                    notok=(q < 3))
            t_ev = b.op(
                "scalar",
                lambda e, s, bk=bk, grp=grp, j=j: e.copy(
                    xn[:, grp * 4:(grp + 1) * 4, j * 128:(j + 1) * 128],
                    pb16[bk][:, 0:512].rearrange("p (a c) -> p a c", a=4)),
                waits=[t_tr])
            tb16_free[bk] = t_ev
            xnslot_consumers[sl].append(t_ev)
            xn_t4_tok[j // 4] = t_ev

    b.barrier()
    p1.close()
    p2 = contextlib.ExitStack()

    def sb2(name, shape, dtype=F32):
        sbufs[name] = p2.enter_context(nc.sbuf_tensor(name, shape, dtype))
        return sbufs[name]

    stg_w = [sb2(f"stg_w{i}", [128, NE, 512], BF16) for i in range(2)]
    wcol = [sb2(f"wcol{i}", [128, NE, 128], BF16) for i in range(2)]
    winp = sb2("winp", [128, T])
    sgrow = sb2("sgrow", [128, T])
    irow = sb2("irow", [128, T])
    drow = [sb2(f"drow{i}", [128, T]) for i in range(2)]
    bprow = [sb2(f"bprow{i}", [128, T]) for i in range(2)]
    gaterow = [sb2(f"gaterow{i}", [128, T]) for i in range(2)]
    strash = sb2("strash", [128, T])
    prow = sb2("prow", [128, T])

    WC = [b.sem("WC0"), b.sem("WC1")]
    SD = [b.sem("SD0"), b.sem("SD1")]
    SBP = [b.sem("SBP0"), b.sem("SBP1")]
    SGA = [b.sem("SGA0"), b.sem("SGA1")]
    wcol_consumer = [None, None]
    row_consumers = {"winp": [], "sg": [], "irow": [], "strash": [], "prow": []}
    drow_cons = [[], []]
    bprow_cons = [[], []]
    gaterow_cons = [[], []]
    mmcnt = [0]

    wsrc = {0: w_in_h, 1: w_dec_h, 2: w_gate_h}
    use = 0
    for e_ in range(NE):
        emit_precast(2 if NE >= 16 else 32 // NE)
        evacs = {}
        for widx in range(3):
            slw = use % 2
            t_w = b.op(
                "gpsimd",
                lambda e, s, widx=widx, e_=e_, slw=slw: e.dma_start(
                    wcol[slw][:], wsrc[widx][e_, :, :, :]),
                waits=[wcol_consumer[slw]], dma=WC[slw])
            use += 1
            evs = []
            bkmap = list(range(NT4))
            mm_last = [None] * NT4
            for t4 in range(NT4):
                bk = bkmap[t4]
                for k in range(NE):
                    mm_last[t4] = b.op(
                        "tensor",
                        lambda e, s, bk=bk, k=k, t4=t4, slw=slw: e.matmul(
                            banks6[bk][:, 0:TT], wcol[slw][:, k, :],
                            xn[:, k, t4 * TT:(t4 + 1) * TT],
                            start=(k == 0), stop=(k == NE - 1)),
                        waits=([t_w, bank_free[bk]] if k == 0 else []),
                        notok=(k < NE - 1))
            for t4 in range(NT4):
                bk = bkmap[t4]
                t_mm = mm_last[t4]
                if widx == 0:
                    t_e1 = b.op("scalar",
                                lambda e, s, bk=bk, t4=t4, e_=e_: e.activation(
                                    winp[:, t4 * TT:(t4 + 1) * TT], banks6[bk][:, 0:TT],
                                    A.Identity, bias=bi_t[:, e_:e_ + 1]),
                                waits=[t_mm] + row_consumers["winp"])
                    t_e2 = b.op("scalar",
                                lambda e, s, bk=bk, t4=t4, e_=e_: e.activation(
                                    sgrow[:, t4 * TT:(t4 + 1) * TT], banks6[bk][:, 0:TT],
                                    A.Sigmoid, bias=bi_t[:, e_:e_ + 1]),
                                waits=row_consumers["sg"])
                    bank_free[bk] = t_e2
                    evs += [t_e1, t_e2]
                elif widx == 1:
                    sld = e_ % 2
                    t_e1 = b.op("scalar",
                                lambda e, s, bk=bk, t4=t4, e_=e_, sld=sld: e.activation(
                                    drow[sld][:, t4 * TT:(t4 + 1) * TT], banks6[bk][:, 0:TT],
                                    A.Sigmoid, bias=bdc_t[:, e_:e_ + 1]),
                                waits=[t_mm, t_bdc] + drow_cons[sld] +
                                      [("SD" + str(sld), b.total["SD" + str(sld)])])
                    if t4 == 0:
                        drow_cons[sld] = []
                    bank_free[bk] = t_e1
                    evs.append(t_e1)
                else:
                    sld = e_ % 2
                    t_e1 = b.op("scalar",
                                lambda e, s, bk=bk, t4=t4, e_=e_, sld=sld: e.activation(
                                    gaterow[sld][:, t4 * TT:(t4 + 1) * TT], banks6[bk][:, 0:TT],
                                    A.Sigmoid, bias=bg_t[:, e_:e_ + 1]),
                                waits=[t_mm] + gaterow_cons[sld] +
                                      [("SGA" + str(sld), b.total["SGA" + str(sld)])])
                    if t4 == 0:
                        gaterow_cons[sld] = []
                    bank_free[bk] = t_e1
                    evs.append(t_e1)
            wcol_consumer[slw] = mm_last[NT4 - 1]
            evacs[widx] = evs

        sld = e_ % 2
        t_i = b.op("vector", lambda e, s: e.tensor_mul(irow[:], winp[:], sgrow[:]),
                   waits=evacs[0] + row_consumers["irow"])
        row_consumers["winp"] = [t_i]
        row_consumers["sg"] = [t_i]
        t_bp = b.op("vector",
                    lambda e, s, sld=sld: e.scalar_tensor_tensor(
                        bprow[sld][:], drow[sld][:], 1.0, irow[:], ALU.subtract, ALU.mult),
                    waits=evacs[1] + [t_i] + bprow_cons[sld] +
                          [("SBP" + str(sld), b.total["SBP" + str(sld)])])
        bprow_cons[sld] = []
        row_consumers["irow"] = [t_bp]
        t_sc = b.op("vector",
                    lambda e, s, sld=sld: e.tensor_tensor_scan(
                        strash[:], drow[sld][:], bprow[sld][:], 0.0, ALU.mult, ALU.subtract),
                    waits=[t_bp] + row_consumers["strash"])
        t_cs = b.op("vector",
                    lambda e, s, e_=e_: e.tensor_copy(pair[:, NE + e_:NE + e_ + 1],
                                                      strash[:, T - 1:T]),
                    waits=[t_sc])
        row_consumers["strash"] = [t_cs]
        t_ps = b.op("vector",
                    lambda e, s, sld=sld: e.tensor_tensor_scan(
                        prow[:], drow[sld][:], drow[sld][:], 1.0, ALU.mult, ALU.bypass),
                    waits=[t_sc] + row_consumers["prow"])
        t_cp = b.op("vector",
                    lambda e, s, e_=e_: e.tensor_copy(pair[:, e_:e_ + 1], prow[:, T - 1:T]),
                    waits=[t_ps])
        row_consumers["prow"] = [t_cp]
        t_sd = b.op("sync", lambda e, s, sld=sld, e_=e_: e.dma_start(sc_d[e_, :, :], drow[sld][:]),
                    waits=[t_ps], dma=SD[sld])
        t_sb_ = b.op("sync", lambda e, s, sld=sld, e_=e_: e.dma_start(sc_bp[e_, :, :], bprow[sld][:]),
                     waits=[t_sc], dma=SBP[sld])
        t_sg_ = b.op("sync", lambda e, s, sld=sld, e_=e_: e.dma_start(sc_gate[e_, :, :], gaterow[sld][:]),
                     waits=evacs[2], dma=SGA[sld])
        drow_cons[sld] = [t_ps]
        bprow_cons[sld] = [t_sc]
        gaterow_cons[sld] = []

    CI = b.sem("CI")
    CCS = b.sem("CCS")
    GS = b.sem("GS")
    pair_done = ("VV", b.total["VV"])
    t_ci = b.op("sync", lambda e, s: e.dma_start(cc_in[:, :], pair[:]),
                waits=[pair_done], dma=CI)

    p2_end = b.snapshot()
    b.barrier(p2_end, exclude=("CI",))
    p2.close()

    p4a = contextlib.ExitStack()

    def sb4(name, shape, dtype=F32):
        sbufs[name] = p4a.enter_context(nc.sbuf_tensor(name, shape, dtype))
        return sbufs[name]

    wgu_g = [sb4(f"wgu_g{i}", [128, NE, 128], BF16) for i in range(2)]
    wgu_u = [sb4(f"wgu_u{i}", [128, NE, 128], BF16) for i in range(2)]
    sg_t = [sb4(f"sg_t{i}", [128, T]) for i in range(2)]
    h_row = [sb4(f"h_row{i}", [128, T], BF16) for i in range(2)]

    WG = [b.sem("WG0"), b.sem("WG1")]
    WU = [b.sem("WU0"), b.sem("WU1")]
    SH = [b.sem("SH0"), b.sem("SH1")]
    wdn_q = [None, None]
    hh = [None, None]
    g_t = [None, None]
    WD = [b.sem("WD0"), b.sem("WD1")]
    WDg = [b.sem("WDg0"), b.sem("WDg1")]
    stgs_tots = [("STGS0", b.total["STGS0"]), ("STGS1", b.total["STGS1"])]
    wd_tok = {}
    wg_consumer = [None, None]
    wu_consumer = [None, None]
    sg_consumer = [None, None]
    hrow_cons = [[], []]

    for i in range(NIC):
        sl = i % 2
        t_wg = b.op("gpsimd",
                    lambda e, s, i=i, sl=sl: e.dma_start(
                        wgu_g[sl][:], w_gu_h[i, :, :, :]),
                    waits=[wg_consumer[sl]], dma=WG[sl])
        t_wu = b.op("gpsimd",
                    lambda e, s, i=i, sl=sl: e.dma_start(
                        wgu_u[sl][:], w_gu_h[NIC + i, :, :, :]),
                    waits=[wu_consumer[sl]], dma=WU[sl])
        h_ops = []
        bkg = list(range(NT4))
        mm_last = [None] * NT4
        for t4 in range(NT4):
            bk = bkg[t4]
            for k in range(NE):
                mm_last[t4] = b.op("tensor",
                                   lambda e, s, bk=bk, k=k, t4=t4, sl=sl: e.matmul(
                                       banks6[bk][:, 0:TT], wgu_g[sl][:, k, :],
                                       xn[:, k, t4 * TT:(t4 + 1) * TT],
                                       start=(k == 0), stop=(k == NE - 1)),
                                   waits=([t_wg, bank_free[bk]] if k == 0 else []),
                                   notok=(k < NE - 1))
        wg_consumer[sl] = mm_last[NT4 - 1]
        sg_evs = []
        for t4 in range(NT4):
            bk = bkg[t4]
            t_sg2 = b.op("scalar",
                         lambda e, s, bk=bk, t4=t4, sl=sl: e.activation(
                             sg_t[sl][:, t4 * TT:(t4 + 1) * TT], banks6[bk][:, 0:TT], A.Silu),
                         waits=[mm_last[t4], sg_consumer[sl]])
            sg_consumer[sl] = None
            bank_free[bk] = t_sg2
            sg_evs.append(t_sg2)
        bku = [4, 5, 0, 1][:NT4]
        mm_lastu = [None] * NT4
        for t4 in range(NT4):
            bk = bku[t4]
            for k in range(NE):
                mm_lastu[t4] = b.op("tensor",
                                    lambda e, s, bk=bk, k=k, t4=t4, sl=sl: e.matmul(
                                        banks6[bk][:, 0:TT], wgu_u[sl][:, k, :],
                                        xn[:, k, t4 * TT:(t4 + 1) * TT],
                                        start=(k == 0), stop=(k == NE - 1)),
                                    waits=([t_wu, bank_free[bk]] if k == 0 else []),
                                    notok=(k < NE - 1))
        wu_consumer[sl] = mm_lastu[NT4 - 1]
        for t4 in range(NT4):
            bk = bku[t4]
            t_h = b.op("vector",
                       lambda e, s, bk=bk, t4=t4, sl=sl: e.tensor_mul(
                           h_row[sl][:, t4 * TT:(t4 + 1) * TT],
                           sg_t[sl][:, t4 * TT:(t4 + 1) * TT], banks6[bk][:, 0:TT]),
                       waits=[mm_lastu[t4], sg_evs[t4]] + (hrow_cons[sl] if t4 == 0 else []))
            if t4 == 0:
                hrow_cons[sl] = []
            bank_free[bk] = t_h
            h_ops.append(t_h)
        sg_consumer[sl] = h_ops[-1]
        t_sh = b.op("sync", lambda e, s, i=i, sl=sl: e.dma_start(sc_h[i, :, :], h_row[sl][:]),
                    waits=h_ops + [("SH" + str(sl), b.total["SH" + str(sl)])], dma=SH[sl])
        hrow_cons[sl] = [t_sh]
        if T == 2048 and i == NIC // 2:
            for part in range(4):
                wd_tok[0] = b.op("scalar",
                                 lambda e, s, part=part: e.dma_start(
                                     wdn_q[0][:, part * 16:(part + 1) * 16, :],
                                     sc_wdn[0, part, :, :, :]),
                                 waits=stgs_tots if part == 0 else [], dma=WD[0])

    p4a_end = b.snapshot()
    mm_p4a = ("MM", b.total["MM"])
    vv_p4a = ("VV", b.total["VV"])
    sc_p4a = ("SC", b.total["SC"])
    sh_tots = [("SH0", b.total["SH0"]), ("SH1", b.total["SH1"])]
    p4a_all = [mm_p4a, vv_p4a, sc_p4a] + sh_tots
    p4a.close()
    xn_stack.close()

    p4b = contextlib.ExitStack()

    def sb4b(name, shape, dtype=F32):
        sbufs[name] = p4b.enter_context(nc.sbuf_tensor(name, shape, dtype))
        return sbufs[name]

    wdn_q[1] = sb4b("wdn_q1", [128, NIC, 512], BF16)
    hh[0] = sb4b("hh0", [128, NIC, TQ], BF16)
    g_t[0] = sb4b("g_t0", [128, 4, TQ])
    g_t[1] = sb4b("g_t1", [128, 4, TQ])
    wdn_q[0] = sb4b("wdn_q0", [128, NIC, 512], BF16)
    hh[1] = sb4b("hh1", [128, NIC, TQ], BF16)

    HH = [b.sem("HH0"), b.sem("HH1")]
    SGSP = [b.sem("SGSP0"), b.sem("SGSP1")]
    hh_consumer = [[], []]
    gt_consumer = [[], []]

    def load_wdn(q, waits, eng="sync"):
        sem = WDg[q % 2] if eng == "gpsimd" else WD[q % 2]
        t_wd = None
        for part in range(4):
            t_wd = b.op(eng,
                        lambda e, s, q=q, part=part: e.dma_start(
                            wdn_q[q % 2][:, part * 16:(part + 1) * 16, :],
                            sc_wdn[q, part, :, :, :]),
                        waits=waits if part == 0 else [], dma=sem)
        return t_wd

    def load_hh(t8, waits):
        sl = t8 % 2
        return b.op("sync",
                    lambda e, s, t8=t8, sl=sl: e.dma_start(
                        hh[sl][:],
                        sc_h.rearrange("i p t -> p i t")[:, :, t8 * TQ:(t8 + 1) * TQ]),
                    waits=waits, dma=HH[sl])

    if T == 2048:
        hh_pre = {0: load_hh(0, [mm_p4a, vv_p4a] + sh_tots)}
        if NT8 > 1:
            hh_pre[1] = load_hh(1, sh_tots)
        wd_tok[1] = load_wdn(1, [mm_p4a], eng="scalar")
    else:
        hh_pre = {0: load_hh(0, p4a_all)}
        if NT8 > 1:
            hh_pre[1] = load_hh(1, [])
        wd_tok[0] = load_wdn(0, p4a_all, eng="scalar")
        wd_tok[1] = load_wdn(1, [], eng="scalar")

    b.barrier(p4a_end, exclude=("CI",))

    t_cc = b.op("gpsimd",
                lambda e, s: e.collective_compute(
                    "AllGather", ALU.bypass,
                    replica_groups=[list(range(NCORES))],
                    ins=[cc_in[:, :]], outs=[cc_out[:, :, :]]),
                waits=[t_ci], inc=("CCS", 1))
    t_gs = b.op("gpsimd",
                lambda e, s: e.dma_start(gsb[:], cc_out.rearrange("r p c -> p r c")),
                waits=[t_cc], dma=GS)

    q2_mm = [None]
    for q in range(4):
        last_mm = None
        for t8 in range(NT8):
            sl = t8 % 2
            if q == 0 and t8 in hh_pre:
                t_hh = hh_pre[t8]
            else:
                t_hh = load_hh(t8, hh_consumer[sl])
            hh_consumer[sl] = []
            gops = []
            for hc in range(4):
                bk = mmcnt[0] % 4
                mmcnt[0] += 1
                t_mm = None
                for ic in range(NIC):
                    t_mm = b.op("tensor",
                                lambda e, s, bk=bk, ic=ic, hc=hc, sl=sl, q=q: e.matmul(
                                    pb[bk][:, 0:TQ], wdn_q[q % 2][:, ic, hc * 128:(hc + 1) * 128],
                                    hh[sl][:, ic, :],
                                    start=(ic == 0), stop=(ic == NIC - 1)),
                                waits=([wd_tok[q], t_hh, bank_free[bk]] if ic == 0 else []),
                                notok=(ic < NIC - 1))
                last_mm = t_mm
                t_gt = b.op("scalar",
                            lambda e, s, bk=bk, hc=hc, sl=sl: e.activation(
                                g_t[sl][:, hc, :], pb[bk][:, 0:TQ], A.Silu),
                            waits=[t_mm] + (gt_consumer[sl] if hc == 0 else []))
                if hc == 0:
                    gt_consumer[sl] = []
                bank_free[bk] = t_gt
                gops.append(t_gt)
                hh_consumer[sl].append(t_mm)
            t_sg3 = b.op("scalar",
                         lambda e, s, q=q, t8=t8, sl=sl: e.dma_start(
                             sc_g.rearrange("f p t -> p f t")[:, q * 4:(q + 1) * 4,
                                                              t8 * TQ:(t8 + 1) * TQ],
                             g_t[sl][:]),
                         waits=gops + [("SGSP" + str(sl), b.total["SGSP" + str(sl)])],
                         dma=SGSP[sl])
            gt_consumer[sl] = [t_sg3]
        if q + 2 <= 3:
            wd_tok[q + 2] = load_wdn(q + 2, [last_mm], eng="gpsimd")
        if q == 2:
            q2_mm[0] = last_mm

    p4b_end = b.snapshot()
    p4b.close()

    p5 = contextlib.ExitStack()

    def sb5(name, shape, dtype=F32):
        sbufs[name] = p5.enter_context(nc.sbuf_tensor(name, shape, dtype))
        return sbufs[name]

    wout = sb5("wout", [128, NE, D], BF16)
    WO2 = b.sem("WO2")
    wo_all = b.op("sync",
                  lambda e, s: e.dma_start(wout[:], sc_wout.rearrange("k p c -> p k c")),
                  waits=[("MM", p4b_end["MM"])], dma=WO2)

    b.barrier(p4b_end)

    d5 = sb5("d5", [128, NE, TQ])
    bp5 = sb5("bp5", [128, NE, TQ])
    ga5 = sb5("ga5", [128, NE, TQ])
    gg5 = sb5("gg5", [128, NE, TQ])
    st5 = sb5("st5", [128, NE, TQ])
    y5 = sb5("y5", [128, NE, TQ], BF16)
    s5 = sb5("s5", [128, NE, TQ])
    x5 = [sb5(f"x5_{i}", [128, D]) for i in range(2)]
    out5 = [sb5(f"out5{i}", [128, D]) for i in range(2)]

    gs_tok = ("GS", b.total["GS"])
    t_h0 = b.op("vector", lambda e, s: e.memset(hcomb[:], 0.0), waits=[gs_tok, l0_tok])
    prev = t_h0
    for jcore in range(NCORES - 1):
        t_a = b.op("vector",
                   lambda e, s, j=jcore: e.tensor_mul(tmp1[:], gsb[:, j, 0:NE], hcomb[:]),
                   waits=[prev])
        t_b2 = b.op("vector",
                    lambda e, s, j=jcore: e.tensor_add(tmp1[:], tmp1[:], gsb[:, j, NE:2 * NE]),
                    waits=[t_a])
        t_c = b.op("vector", lambda e, s: e.tensor_sub(tmp2[:], tmp1[:], hcomb[:]),
                   waits=[t_b2])
        t_d2 = b.op("vector",
                    lambda e, s, j=jcore: e.tensor_scalar_mul(tmp2[:], tmp2[:],
                                                              cmask[:, j:j + 1]),
                    waits=[t_c])
        prev = b.op("vector", lambda e, s: e.tensor_add(hcomb[:], hcomb[:], tmp2[:]),
                    waits=[t_d2])
    t_carry0 = b.op("vector", lambda e, s: e.tensor_copy(carry[:], hcomb[:]), waits=[prev])

    LD5 = b.sem("LD5")
    LB5 = b.sem("LB5")
    LG5 = b.sem("LG5")
    LGG = b.sem("LGG")
    LX5 = [b.sem("LX5a"), b.sem("LX5b")]
    SOUT = [b.sem("SOUT0"), b.sem("SOUT1")]
    y5_cons = []
    out5_cons = [[], []]
    carry_last = [t_carry0]

    def ld_sc(eng, sem, dst, src, t8, waits):
        return b.op(eng,
                    lambda e, s, t8=t8, dst=dst, src=src: e.dma_start(
                        dst[:], src.rearrange("f p t -> p f t")[:, :, t8 * TQ:(t8 + 1) * TQ]),
                    waits=waits, dma=sem)

    t_ld = ld_sc("sync", LD5, d5, sc_d, 0, [])
    t_lb = ld_sc("sync", LB5, bp5, sc_bp, 0, [])
    t_lg = ld_sc("gpsimd", LG5, ga5, sc_gate, 0, [])
    t_lgg = ld_sc("sync", LGG, gg5, sc_g, 0, [])

    t_x5 = [b.op("gpsimd",
                 lambda e, s, j2=j2: e.dma_start(x5[j2][:], x_h[j2 * 128:(j2 + 1) * 128, :]),
                 dma=LX5[j2]) for j2 in range(2)]

    def emit_scans(t8, t_ld, t_lb, first_extra=()):
        sc = []
        prev = carry_last[0]
        for fc in range(NE):
            t_s5 = b.op("vector",
                        lambda e, s, fc=fc, t8=t8: e.tensor_tensor_scan(
                            st5[:, fc, :], d5[:, fc, :], bp5[:, fc, :],
                            carry[:, fc:fc + 1], ALU.mult, ALU.subtract),
                        waits=([t_ld, t_lb, prev] + list(first_extra)) if fc == 0
                              else [sc[fc - 1]])
            sc.append(t_s5)
        t_cy = b.op("vector",
                    lambda e, s: e.tensor_copy(
                        carry[:, 0:NE], st5[:, :, TQ - 1:TQ].rearrange("p f o -> p (f o)")),
                    waits=[sc[-1]])
        carry_last[0] = t_cy
        return sc

    scans = emit_scans(0, t_ld, t_lb)
    pend = None
    for t8 in range(NT8):
        t_y = b.op("vector", lambda e, s: e.tensor_mul(y5[:], ga5[:], st5[:]),
                   waits=[t_lg] + scans + y5_cons)
        y5_cons = []
        if t8 + 1 < NT8:
            t_ldn = ld_sc("sync", LD5, d5, sc_d, t8 + 1, [scans[-1]])
            t_lbn = ld_sc("sync", LB5, bp5, sc_bp, t8 + 1, [scans[-1]])
            t_lgn = ld_sc("gpsimd", LG5, ga5, sc_gate, t8 + 1, [t_y])
            scans = emit_scans(t8 + 1, t_ldn, t_lbn, first_extra=[t_y])
        if pend is not None:
            pt8, pcps, px5 = pend
            adds = []
            for j2 in range(TQ // 128):
                row0 = pt8 * TQ + j2 * 128
                t_ad = b.op("vector",
                            lambda e, s, j2=j2: e.tensor_add(
                                out5[j2][:], out5[j2][:], x5[j2][:]),
                            waits=pcps[j2] + [px5[j2]])
                adds.append(t_ad)
                t_st = b.op("sync",
                            lambda e, s, row0=row0, j2=j2: e.dma_start(
                                out_h[row0:row0 + 128, :], out5[j2][:]),
                            waits=[t_ad], dma=SOUT[j2])
                out5_cons[j2] = [t_st]
            t_x5 = [b.op("gpsimd",
                         lambda e, s, j2=j2, t8=t8: e.dma_start(
                             x5[j2][:], x_h[t8 * TQ + j2 * 128:t8 * TQ + (j2 + 1) * 128, :]),
                         waits=[adds[j2]], dma=LX5[j2])
                    for j2 in range(2)]
        ev5 = []
        for e_ in range(NE):
            bk = e_ % 2
            t_mm = None
            for k in range(NE):
                t_mm = b.op("tensor",
                            lambda e, s, bk=bk, k=k, e_=e_: e.matmul(
                                pb[bk][:, 0:TQ], wout[:, k, e_ * 128:(e_ + 1) * 128],
                                y5[:, k, :], start=(k == 0), stop=(k == NE - 1)),
                            waits=([wo_all, t_y, bank_free[bk]]
                                   if k == 0 else []),
                            notok=(k < NE - 1))
            y5_cons = [t_mm]
            t_e5 = b.op("scalar",
                        lambda e, s, bk=bk, e_=e_: e.activation(
                            s5[:, e_, :], pb[bk][:, 0:TQ], A.Identity,
                            bias=bo_t[:, e_:e_ + 1]),
                        waits=[t_mm])
            bank_free[bk] = t_e5
            ev5.append(t_e5)
        t_rg = []
        for g4 in range(4):
            t_rg.append(b.op("vector",
                             lambda e, s, g4=g4: e.tensor_mul(
                                 s5[:, g4 * 4:(g4 + 1) * 4, :],
                                 s5[:, g4 * 4:(g4 + 1) * 4, :],
                                 gg5[:, g4 * 4:(g4 + 1) * 4, :]),
                             waits=[ev5[g4 * 4 + 3], t_lgg] if g4 == 0
                                   else [ev5[g4 * 4 + 3]]))
        if t8 + 1 < NT8:
            t_lggn = ld_sc("sync", LGG, gg5, sc_g, t8 + 1, [t_rg[-1]])
        cps = {0: [], 1: []}
        for j2 in range(TQ // 128):
            for grp in range(4):
                bk = (j2 * 4 + grp) % 2
                t_tr = None
                for qq in range(4):
                    fc = grp * 4 + qq
                    t_tr = b.op("tensor",
                                lambda e, s, bk=bk, qq=qq, fc=fc, j2=j2: e.transpose(
                                    banks6[4 + bk][:, qq * 128:(qq + 1) * 128],
                                    s5[:, fc, j2 * 128:(j2 + 1) * 128], eye32[:]),
                                waits=([t_rg[grp], bank_free[4 + bk], l0_tok]
                                       if qq == 0 else []),
                                notok=(qq < 3))
                t_cp = b.op("scalar",
                            lambda e, s, bk=bk, grp=grp, j2=j2: e.copy(
                                out5[j2][:, grp * 512:(grp + 1) * 512],
                                banks6[4 + bk][:, 0:512]),
                            waits=[t_tr] + (out5_cons[j2] if grp == 0 else []))
                if grp == 0:
                    out5_cons[j2] = []
                bank_free[4 + bk] = t_cp
                cps[j2].append(t_cp)
        pend = (t8, cps, t_x5)
        if t8 + 1 < NT8:
            t_ld, t_lb, t_lg, t_lgg = t_ldn, t_lbn, t_lgn, t_lggn

    pt8, pcps, px5 = pend
    for j2 in range(TQ // 128):
        row0 = pt8 * TQ + j2 * 128
        t_ad = b.op("vector",
                    lambda e, s, j2=j2: e.tensor_add(out5[j2][:], out5[j2][:], x5[j2][:]),
                    waits=pcps[j2] + [px5[j2]])
        t_st = b.op("sync",
                    lambda e, s, row0=row0, j2=j2: e.dma_start(
                        out_h[row0:row0 + 128, :], out5[j2][:]),
                    waits=[t_ad], dma=SOUT[j2])
        out5_cons[j2] = [t_st]

    b.wait("sync", [("SOUT0", b.total["SOUT0"]), ("SOUT1", b.total["SOUT1"])])

    sem_handles = {}
    for name in b.sem_names:
        sem_handles[name] = es.enter_context(nc.semaphore(name))
    with nc.Block() as blk:
        @blk.tensor
        def _(e):
            b.replay("tensor", e, sem_handles)

        @blk.scalar
        def _(e):
            b.replay("scalar", e, sem_handles)

        @blk.vector
        def _(e):
            b.replay("vector", e, sem_handles)

        @blk.gpsimd
        def _(e):
            b.replay("gpsimd", e, sem_handles)

        @blk.sync
        def _(e):
            b.replay("sync", e, sem_handles)
    p5.close()
    es.close()
    return nc


_NC_CACHE = {}


def _get_nc(T):
    if T not in _NC_CACHE:
        _NC_CACHE[T] = build_nc(T)
    return _NC_CACHE[T]


def _bias_tab(v):
    return np.ascontiguousarray(np.asarray(v, dtype=np.float32).reshape(NE, 128).T)


def _attn_blocks(w):
    w = np.asarray(w, dtype=np.float32)
    return np.ascontiguousarray(w.reshape(NE, 128, NE, 128).transpose(2, 1, 0, 3))


def kernel(L0_attn_x, L0_attn_W_in, L0_attn_b_in, L0_attn_W_decay, L0_attn_b_decay,
           L0_attn_W_gate, L0_attn_b_gate, L0_attn_W_out, L0_attn_b_out,
           L0_attn_decay_bias, L0_ffn_w_gate_up, L0_ffn_w_down):
    T = SEQ // NCORES
    x = np.ascontiguousarray(np.asarray(L0_attn_x, dtype=np.float32).reshape(SEQ, D))
    w_gu = np.asarray(L0_ffn_w_gate_up, dtype=np.float32)
    w_gu_r = np.ascontiguousarray(
        w_gu.reshape(2 * NIC, 128, NE, 128).transpose(0, 3, 2, 1))
    w_dn = np.asarray(L0_ffn_w_down, dtype=np.float32)
    w_dn_r = np.ascontiguousarray(
        w_dn.T.reshape(4, 16, 128, 4, 512).transpose(3, 0, 2, 1, 4))
    w_out = np.ascontiguousarray(np.asarray(L0_attn_W_out, dtype=np.float32))
    base = dict(
        w_in_r=_attn_blocks(L0_attn_W_in), w_dec_r=_attn_blocks(L0_attn_W_decay),
        w_gate_r=_attn_blocks(L0_attn_W_gate), w_out=w_out,
        w_gu_r=w_gu_r, w_dn_r=w_dn_r,
        b_in_t=_bias_tab(L0_attn_b_in), b_dec_t=_bias_tab(L0_attn_b_decay),
        b_db_t=_bias_tab(L0_attn_decay_bias), b_gate_t=_bias_tab(L0_attn_b_gate),
        b_out_t=_bias_tab(L0_attn_b_out),
        eye32=np.eye(128, dtype=np.float32),
    )
    in_maps = []
    for c in range(NCORES):
        m = dict(base)
        m["x"] = np.ascontiguousarray(x[c * T:(c + 1) * T, :])
        cm = np.zeros((128, NCORES), dtype=np.float32)
        cm[:, :c] = 1.0
        m["cmask"] = cm
        in_maps.append(m)
    nc = _get_nc(T)
    res = run_bass_kernel_spmd(nc, in_maps, core_ids=list(range(NCORES)))
    out = np.concatenate([np.asarray(r["out"]) for r in res.results], axis=0)
    return out.reshape(1, SEQ, D).astype(np.float32)


# version 14
# speedup vs baseline: 1.0085x; 1.0031x over previous
import os
os.environ.setdefault("BASS_INTERP_NUM_WORKERS", "8")

import contextlib
import numpy as np

import concourse.bass as bass
import concourse.mybir as mybir
from concourse.bass_utils import run_bass_kernel_spmd

F32 = mybir.dt.float32
BF16 = mybir.dt.bfloat16
A = mybir.ActivationFunctionType
ALU = mybir.AluOpType
AX = mybir.AxisListType

D = 2048
NE = D // 128
INTER = 8192
NIC = INTER // 128
SEQ = 16384
NCORES = 8
EPS = 1e-5


class KB:

    ENGS = ("tensor", "scalar", "vector", "gpsimd", "sync")

    def __init__(self):
        self.recs = {e: [] for e in self.ENGS}
        self.sem_names = []
        self.total = {}
        self.selfsem = {"tensor": "MM", "scalar": "SC", "vector": "VV"}
        for s in ("MM", "SC", "VV"):
            self._reg(s)

    def _reg(self, name):
        if name not in self.total:
            self.sem_names.append(name)
            self.total[name] = 0

    def sem(self, name):
        self._reg(name)
        return name

    def op(self, eng, fn, waits=(), dma=None, inc=None, notok=False):
        waits = [w for w in waits if w is not None]
        if notok:
            self.recs[eng].append((fn, waits, None))
            return None
        if dma is not None:
            self._reg(dma)
            self.total[dma] += 16
            tok = (dma, self.total[dma])
            self.recs[eng].append((fn, waits, (dma, 16)))
            return tok
        if inc is not None:
            s, n = inc
            self._reg(s)
            self.total[s] += n
            tok = (s, self.total[s])
            self.recs[eng].append((fn, waits, (s, n)))
            return tok
        if eng in self.selfsem:
            s = self.selfsem[eng]
            self.total[s] += 1
            tok = (s, self.total[s])
            self.recs[eng].append((fn, waits, (s, 1)))
            return tok
        self.recs[eng].append((fn, waits, None))
        return None

    def wait(self, eng, toks):
        toks = [t for t in toks if t is not None]
        if toks:
            self.recs[eng].append((None, toks, None))

    def snapshot(self):
        return dict(self.total)

    def barrier(self, snap=None, exclude=()):
        snap = snap or self.snapshot()
        toks = [(s, v) for s, v in snap.items() if v > 0 and s not in exclude]
        for e in self.ENGS:
            self.wait(e, toks)

    def replay(self, eng_name, eng, sems):
        waited = {}
        for fn, waits, inc in self.recs[eng_name]:
            for (s, v) in waits:
                if v > 0 and waited.get(s, -1) < v:
                    eng.wait_ge(sems[s], v)
                    waited[s] = v
            if fn is not None:
                instr = fn(eng, sems)
                if inc is not None and instr is not None:
                    instr.then_inc(sems[inc[0]], inc[1])


def build_nc(T):
    NJ = T // 128
    TT = min(512, T)
    NT4 = T // TT
    TQ = min(256, T)
    NT8 = T // TQ

    nc = bass.Bass("TRN2", target_bir_lowering=False)

    x_h = nc.dram_tensor("x", [T, D], F32, kind="ExternalInput").ap()
    w_in_h = nc.dram_tensor("w_in_r", [NE, 128, NE, 128], F32, kind="ExternalInput").ap()
    w_dec_h = nc.dram_tensor("w_dec_r", [NE, 128, NE, 128], F32, kind="ExternalInput").ap()
    w_gate_h = nc.dram_tensor("w_gate_r", [NE, 128, NE, 128], F32, kind="ExternalInput").ap()
    w_out_h = nc.dram_tensor("w_out", [D, D], F32, kind="ExternalInput").ap()
    w_gu_h = nc.dram_tensor("w_gu_r", [2 * NIC, 128, NE, 128], F32, kind="ExternalInput").ap()
    w_dn_h = nc.dram_tensor("w_dn_r", [4, 4, 128, 16, 512], F32, kind="ExternalInput").ap()
    b_in_h = nc.dram_tensor("b_in_t", [128, NE], F32, kind="ExternalInput").ap()
    b_dec_h = nc.dram_tensor("b_dec_t", [128, NE], F32, kind="ExternalInput").ap()
    b_db_h = nc.dram_tensor("b_db_t", [128, NE], F32, kind="ExternalInput").ap()
    b_gate_h = nc.dram_tensor("b_gate_t", [128, NE], F32, kind="ExternalInput").ap()
    b_out_h = nc.dram_tensor("b_out_t", [128, NE], F32, kind="ExternalInput").ap()
    eye_h = nc.dram_tensor("eye32", [128, 128], F32, kind="ExternalInput").ap()
    mask_h = nc.dram_tensor("cmask", [128, NCORES], F32, kind="ExternalInput").ap()
    out_h = nc.dram_tensor("out", [T, D], F32, kind="ExternalOutput").ap()

    sc_d = nc.dram_tensor("sc_d", [NE, 128, T], F32, kind="Internal").ap()
    sc_bp = nc.dram_tensor("sc_bp", [NE, 128, T], F32, kind="Internal").ap()
    sc_gate = nc.dram_tensor("sc_gate", [NE, 128, T], F32, kind="Internal").ap()
    sc_g = nc.dram_tensor("sc_g", [NE, 128, T], F32, kind="Internal").ap()
    sc_h = nc.dram_tensor("sc_h", [NIC, 128, T], BF16, kind="Internal").ap()
    cc_in = nc.dram_tensor("cc_in", [128, 2 * NE], F32, kind="Internal").ap()
    cc_out = nc.dram_tensor("cc_out", [NCORES, 128, 2 * NE], F32, kind="Internal").ap()
    sc_wdn = nc.dram_tensor("sc_wdn", [4, 4, 128, 16, 512], BF16, kind="Internal").ap()
    sc_wout = nc.dram_tensor("sc_wout", [NE, 128, D], BF16, kind="Internal").ap()

    b = KB()
    es = contextlib.ExitStack()
    sbufs = {}

    def sb(name, shape, dtype=F32):
        sbufs[name] = es.enter_context(nc.sbuf_tensor(name, shape, dtype))
        return sbufs[name]

    bi_t = sb("bi_t", [128, NE])
    bdc_t = sb("bdc_t", [128, NE])
    bdb_t = sb("bdb_t", [128, NE])
    bg_t = sb("bg_t", [128, NE])
    bo_t = sb("bo_t", [128, NE])
    eye32 = sb("eye32s", [128, 128])
    eye16 = sb("eye16s", [128, 128], BF16)
    cmask = sb("cmasks", [128, NCORES])
    pair = sb("pair", [128, 2 * NE])
    gsb = sb("gsb", [128, NCORES, 2 * NE])
    hcomb = sb("hcomb", [128, NE])
    tmp1 = sb("tmp1", [128, NE])
    tmp2 = sb("tmp2", [128, NE])
    carry = sb("carry", [128, NE])

    pb = [es.enter_context(nc.psum_tensor(f"pb{i}", [128, 512], F32)) for i in range(4)]
    pb16 = [es.enter_context(nc.psum_tensor(f"pb16_{i}", [128, 1024], BF16)) for i in range(2)]
    pb32 = [es.enter_context(nc.psum_tensor(f"pb32_{i}", [128, 512], F32)) for i in range(2)]
    banks6 = pb + pb32
    bank_free = [None] * 6
    tb16_free = [None] * 2
    ring = [0]

    def ring_next():
        bk = ring[0] % 6
        ring[0] += 1
        return bk

    xn_stack = contextlib.ExitStack()
    xn = xn_stack.enter_context(nc.sbuf_tensor("xn", [128, NE, T], BF16))

    L0 = b.sem("L0")
    for dst, src in ((bi_t, b_in_h), (bdc_t, b_dec_h), (bdb_t, b_db_h),
                     (bg_t, b_gate_h), (bo_t, b_out_h), (eye32, eye_h),
                     (cmask, mask_h)):
        b.op("sync", lambda e, s, dst=dst, src=src: e.dma_start(dst[:], src[:, :]),
             dma="L0")
    l0_tok = ("L0", b.total["L0"])
    t_eye16 = b.op("scalar", lambda e, s: e.copy(eye16[:], eye32[:]), waits=[l0_tok])
    t_bdc = b.op("vector", lambda e, s: e.tensor_add(bdc_t[:], bdc_t[:], bdb_t[:]),
                 waits=[l0_tok])

    p1 = contextlib.ExitStack()

    def sb1(name, shape, dtype=F32):
        sbufs[name] = p1.enter_context(nc.sbuf_tensor(name, shape, dtype))
        return sbufs[name]

    NSL = 4
    x_t = [sb1(f"x_t{i}", [128, D]) for i in range(NSL)]
    xn_tm = [sb1(f"xn_tm{i}", [128, D], BF16) for i in range(NSL)]
    st2 = [{k: sb1(f"st{i}_" + k, [128, 1]) for k in
            ("sx", "sq", "mean", "msq", "m2", "vv1", "vp", "s1", "r0",
             "aa", "bb", "cc", "rr", "nb")} for i in range(NSL)]
    xsqs = [sb1(f"xsqs{i}", [128, D]) for i in range(NSL)]

    STGL = [b.sem("STGL0"), b.sem("STGL1")]
    STGS = [b.sem("STGS0"), b.sem("STGS1")]
    precast_jobs = []
    for q in range(4):
        for part in range(4):
            precast_jobs.append((
                lambda sl, q=q, part=part: (stg_w[sl][:, :, :], w_dn_h[q, part, :, :, :]),
                lambda sl, q=q, part=part: (sc_wdn[q, part, :, :, :], stg_w[sl][:, :, :])))
    for k in range(NE):
        precast_jobs.append((
            lambda sl, k=k: (stg_w[sl][:, 0:4, :].rearrange("p a c -> p (a c)"),
                             w_out_h[k * 128:(k + 1) * 128, :]),
            lambda sl, k=k: (sc_wout[k, :, :],
                             stg_w[sl][:, 0:4, :].rearrange("p a c -> p (a c)"))))
    pc_i = [0]
    pc_store_tok = [None, None]
    xn_t4_tok = {}

    def emit_precast(njobs):
        for _ in range(njobs):
            i = pc_i[0]
            if i >= len(precast_jobs): return
            pc_i[0] += 1
            sl = i % 2
            ldfn, stfn = precast_jobs[i]
            t_l = b.op("gpsimd",
                       lambda e, s, ldfn=ldfn, sl=sl: e.dma_start(*ldfn(sl)),
                       waits=[pc_store_tok[sl]], dma=STGL[sl])
            pc_store_tok[sl] = b.op("sync",
                                    lambda e, s, stfn=stfn, sl=sl: e.dma_start(*stfn(sl)),
                                    waits=[t_l], dma=STGS[sl])

    LX = [b.sem(f"LX{i}") for i in range(NSL)]
    xslot_consumers = [[] for _ in range(NSL)]
    xnslot_consumers = [[] for _ in range(NSL)]
    xsq_consumer = [None] * NSL
    prev_stats = [None] * NSL

    for j in range(NJ):
        sl = j % NSL
        t_load = b.op(
            "sync",
            lambda e, s, sl=sl, j=j: e.dma_start(x_t[sl][:], x_h[j * 128:(j + 1) * 128, :]),
            waits=xslot_consumers[sl], dma=LX[sl])
        xslot_consumers[sl] = []

        st_ = st2[sl]
        myxsq = xsqs[sl]
        t_rs1 = b.op("scalar",
                     lambda e, s, sl=sl, myxsq=myxsq, st_=st_: e.activation(
                         myxsq[:], x_t[sl][:], A.Identity, accum_out=st_["sx"][:]),
                     waits=[t_load, prev_stats[sl], xsq_consumer[sl]])
        t_rs2 = b.op("scalar",
                     lambda e, s, sl=sl, myxsq=myxsq, st_=st_: e.activation(
                         myxsq[:], x_t[sl][:], A.Square, accum_out=st_["sq"][:]),
                     waits=[t_rs1])
        t_sq = t_rs2
        xsq_consumer[sl] = t_rs2
        inv = 1.0 / D
        t_m = b.op("vector", lambda e, s, st_=st_: e.tensor_scalar_mul(st_["mean"][:], st_["sx"][:], inv),
                   waits=[t_rs1])
        t_mq = b.op("vector", lambda e, s, st_=st_: e.tensor_scalar_mul(st_["msq"][:], st_["sq"][:], inv),
                    waits=[t_rs2])
        t_m2 = b.op("vector", lambda e, s, st_=st_: e.tensor_mul(st_["m2"][:], st_["mean"][:], st_["mean"][:]),
                    waits=[t_m])
        t_v1 = b.op("vector", lambda e, s, st_=st_: e.tensor_sub(st_["vv1"][:], st_["msq"][:], st_["m2"][:]),
                    waits=[t_mq, t_m2])
        t_vp = b.op("vector", lambda e, s, st_=st_: e.tensor_scalar_add(st_["vp"][:], st_["vv1"][:], EPS),
                    waits=[t_v1])
        t_s1 = b.op("scalar", lambda e, s, st_=st_: e.activation(st_["s1"][:], st_["vp"][:], A.Sqrt),
                    waits=[t_vp])
        t_r0 = b.op("vector", lambda e, s, st_=st_: e.reciprocal(st_["r0"][:], st_["s1"][:]),
                    waits=[t_s1])
        t_aa = b.op("vector", lambda e, s, st_=st_: e.tensor_mul(st_["aa"][:], st_["r0"][:], st_["r0"][:]),
                    waits=[t_r0])
        t_bb = b.op("vector", lambda e, s, st_=st_: e.tensor_mul(st_["bb"][:], st_["aa"][:], st_["vp"][:]),
                    waits=[t_aa, t_vp])
        t_cc = b.op("vector", lambda e, s, st_=st_: e.tensor_scalar(st_["cc"][:], st_["bb"][:], -0.5, 1.5,
                                                           ALU.mult, ALU.add),
                    waits=[t_bb])
        t_rr = b.op("vector", lambda e, s, st_=st_: e.tensor_mul(st_["rr"][:], st_["r0"][:], st_["cc"][:]),
                    waits=[t_cc])
        t_nb = b.op("vector", lambda e, s, st_=st_: e.tensor_scalar(st_["nb"][:], st_["mean"][:],
                                                           st_["rr"][:, 0:1], -1.0,
                                                           ALU.mult, ALU.mult),
                    waits=[t_rr])
        t_xn = b.op("scalar",
                    lambda e, s, sl=sl, st_=st_: e.activation(xn_tm[sl][:], x_t[sl][:], A.Identity,
                                                              bias=st_["nb"][:, 0:1],
                                                              scale=st_["rr"][:, 0:1]),
                    waits=[t_nb, xnslot_consumers[sl][-1] if xnslot_consumers[sl] else None])
        prev_stats[sl] = t_xn
        xnslot_consumers[sl] = []
        xslot_consumers[sl] = [t_sq, t_rs1, t_xn]
        for grp in range(4):
            bk = (j * 4 + grp) % 2
            t_tr = None
            for q in range(4):
                fc = grp * 4 + q
                t_tr = b.op(
                    "tensor",
                    lambda e, s, bk=bk, q=q, fc=fc, sl=sl: e.transpose(
                        pb16[bk][:, q * 128:(q + 1) * 128],
                        xn_tm[sl][:, fc * 128:(fc + 1) * 128], eye16[:]),
                    waits=([t_xn, t_eye16, tb16_free[bk]] if q == 0 else []),
                    notok=(q < 3))
            t_ev = b.op(
                "scalar",
                lambda e, s, bk=bk, grp=grp, j=j: e.copy(
                    xn[:, grp * 4:(grp + 1) * 4, j * 128:(j + 1) * 128],
                    pb16[bk][:, 0:512].rearrange("p (a c) -> p a c", a=4)),
                waits=[t_tr])
            tb16_free[bk] = t_ev
            xnslot_consumers[sl].append(t_ev)
            xn_t4_tok[j // 4] = t_ev

    b.barrier()
    p1.close()
    p2 = contextlib.ExitStack()

    def sb2(name, shape, dtype=F32):
        sbufs[name] = p2.enter_context(nc.sbuf_tensor(name, shape, dtype))
        return sbufs[name]

    stg_w = [sb2(f"stg_w{i}", [128, NE, 512], BF16) for i in range(2)]
    wcol = [sb2(f"wcol{i}", [128, NE, 128], BF16) for i in range(2)]
    winp = sb2("winp", [128, T])
    sgrow = sb2("sgrow", [128, T])
    irow = sb2("irow", [128, T])
    drow = [sb2(f"drow{i}", [128, T]) for i in range(2)]
    bprow = [sb2(f"bprow{i}", [128, T]) for i in range(2)]
    gaterow = [sb2(f"gaterow{i}", [128, T]) for i in range(2)]
    strash = sb2("strash", [128, T])
    prow = sb2("prow", [128, T])

    WC = [b.sem("WC0"), b.sem("WC1")]
    SD = [b.sem("SD0"), b.sem("SD1")]
    SBP = [b.sem("SBP0"), b.sem("SBP1")]
    SGA = [b.sem("SGA0"), b.sem("SGA1")]
    wcol_consumer = [None, None]
    row_consumers = {"winp": [], "sg": [], "irow": [], "strash": [], "prow": []}
    drow_cons = [[], []]
    bprow_cons = [[], []]
    gaterow_cons = [[], []]
    mmcnt = [0]

    wsrc = {0: w_in_h, 1: w_dec_h, 2: w_gate_h}
    use = 0
    for e_ in range(NE):
        emit_precast(2 if NE >= 16 else 32 // NE)
        evacs = {}
        for widx in range(3):
            slw = use % 2
            t_w = b.op(
                "gpsimd",
                lambda e, s, widx=widx, e_=e_, slw=slw: e.dma_start(
                    wcol[slw][:], wsrc[widx][e_, :, :, :]),
                waits=[wcol_consumer[slw]], dma=WC[slw])
            use += 1
            evs = []
            bkmap = list(range(NT4))
            mm_last = [None] * NT4
            for t4 in range(NT4):
                bk = bkmap[t4]
                for k in range(NE):
                    mm_last[t4] = b.op(
                        "tensor",
                        lambda e, s, bk=bk, k=k, t4=t4, slw=slw: e.matmul(
                            banks6[bk][:, 0:TT], wcol[slw][:, k, :],
                            xn[:, k, t4 * TT:(t4 + 1) * TT],
                            start=(k == 0), stop=(k == NE - 1)),
                        waits=([t_w, bank_free[bk]] if k == 0 else []),
                        notok=(k < NE - 1))
            for t4 in range(NT4):
                bk = bkmap[t4]
                t_mm = mm_last[t4]
                if widx == 0:
                    t_e1 = b.op("scalar",
                                lambda e, s, bk=bk, t4=t4, e_=e_: e.activation(
                                    winp[:, t4 * TT:(t4 + 1) * TT], banks6[bk][:, 0:TT],
                                    A.Identity, bias=bi_t[:, e_:e_ + 1]),
                                waits=[t_mm] + row_consumers["winp"])
                    t_e2 = b.op("scalar",
                                lambda e, s, bk=bk, t4=t4, e_=e_: e.activation(
                                    sgrow[:, t4 * TT:(t4 + 1) * TT], banks6[bk][:, 0:TT],
                                    A.Sigmoid, bias=bi_t[:, e_:e_ + 1]),
                                waits=row_consumers["sg"])
                    bank_free[bk] = t_e2
                    evs += [t_e1, t_e2]
                elif widx == 1:
                    sld = e_ % 2
                    t_e1 = b.op("scalar",
                                lambda e, s, bk=bk, t4=t4, e_=e_, sld=sld: e.activation(
                                    drow[sld][:, t4 * TT:(t4 + 1) * TT], banks6[bk][:, 0:TT],
                                    A.Sigmoid, bias=bdc_t[:, e_:e_ + 1]),
                                waits=[t_mm, t_bdc] + drow_cons[sld] +
                                      [("SD" + str(sld), b.total["SD" + str(sld)])])
                    if t4 == 0:
                        drow_cons[sld] = []
                    bank_free[bk] = t_e1
                    evs.append(t_e1)
                else:
                    sld = e_ % 2
                    t_e1 = b.op("scalar",
                                lambda e, s, bk=bk, t4=t4, e_=e_, sld=sld: e.activation(
                                    gaterow[sld][:, t4 * TT:(t4 + 1) * TT], banks6[bk][:, 0:TT],
                                    A.Sigmoid, bias=bg_t[:, e_:e_ + 1]),
                                waits=[t_mm] + gaterow_cons[sld] +
                                      [("SGA" + str(sld), b.total["SGA" + str(sld)])])
                    if t4 == 0:
                        gaterow_cons[sld] = []
                    bank_free[bk] = t_e1
                    evs.append(t_e1)
            wcol_consumer[slw] = mm_last[NT4 - 1]
            evacs[widx] = evs

        sld = e_ % 2
        t_i = b.op("vector", lambda e, s: e.tensor_mul(irow[:], winp[:], sgrow[:]),
                   waits=evacs[0] + row_consumers["irow"])
        row_consumers["winp"] = [t_i]
        row_consumers["sg"] = [t_i]
        t_bp = b.op("vector",
                    lambda e, s, sld=sld: e.scalar_tensor_tensor(
                        bprow[sld][:], drow[sld][:], 1.0, irow[:], ALU.subtract, ALU.mult),
                    waits=evacs[1] + [t_i] + bprow_cons[sld] +
                          [("SBP" + str(sld), b.total["SBP" + str(sld)])])
        bprow_cons[sld] = []
        row_consumers["irow"] = [t_bp]
        t_sc = b.op("vector",
                    lambda e, s, sld=sld: e.tensor_tensor_scan(
                        strash[:], drow[sld][:], bprow[sld][:], 0.0, ALU.mult, ALU.subtract),
                    waits=[t_bp] + row_consumers["strash"])
        t_cs = b.op("vector",
                    lambda e, s, e_=e_: e.tensor_copy(pair[:, NE + e_:NE + e_ + 1],
                                                      strash[:, T - 1:T]),
                    waits=[t_sc])
        row_consumers["strash"] = [t_cs]
        t_ps = b.op("vector",
                    lambda e, s, sld=sld: e.tensor_tensor_scan(
                        prow[:], drow[sld][:], drow[sld][:], 1.0, ALU.mult, ALU.bypass),
                    waits=[t_sc] + row_consumers["prow"])
        t_cp = b.op("vector",
                    lambda e, s, e_=e_: e.tensor_copy(pair[:, e_:e_ + 1], prow[:, T - 1:T]),
                    waits=[t_ps])
        row_consumers["prow"] = [t_cp]
        t_sd = b.op("sync", lambda e, s, sld=sld, e_=e_: e.dma_start(sc_d[e_, :, :], drow[sld][:]),
                    waits=[t_ps], dma=SD[sld])
        t_sb_ = b.op("sync", lambda e, s, sld=sld, e_=e_: e.dma_start(sc_bp[e_, :, :], bprow[sld][:]),
                     waits=[t_sc], dma=SBP[sld])
        t_sg_ = b.op("sync", lambda e, s, sld=sld, e_=e_: e.dma_start(sc_gate[e_, :, :], gaterow[sld][:]),
                     waits=evacs[2], dma=SGA[sld])
        drow_cons[sld] = [t_ps]
        bprow_cons[sld] = [t_sc]
        gaterow_cons[sld] = []

    CI = b.sem("CI")
    CCS = b.sem("CCS")
    GS = b.sem("GS")
    pair_done = ("VV", b.total["VV"])
    t_ci = b.op("sync", lambda e, s: e.dma_start(cc_in[:, :], pair[:]),
                waits=[pair_done], dma=CI)

    p2_end = b.snapshot()
    b.barrier(p2_end, exclude=("CI",))
    p2.close()

    p4a = contextlib.ExitStack()

    def sb4(name, shape, dtype=F32):
        sbufs[name] = p4a.enter_context(nc.sbuf_tensor(name, shape, dtype))
        return sbufs[name]

    wgu_g = [sb4(f"wgu_g{i}", [128, NE, 128], BF16) for i in range(2)]
    wgu_u = [sb4(f"wgu_u{i}", [128, NE, 128], BF16) for i in range(2)]
    sg_t = [sb4(f"sg_t{i}", [128, T]) for i in range(2)]
    h_row = [sb4(f"h_row{i}", [128, T], BF16) for i in range(2)]

    WG = [b.sem("WG0"), b.sem("WG1")]
    WU = [b.sem("WU0"), b.sem("WU1")]
    SH = [b.sem("SH0"), b.sem("SH1")]
    wdn_q = [None, None]
    hh = [None, None]
    g_t = [None, None]
    sh_tok_by_i = {}
    hh0_pre_toks = []
    WD = [b.sem("WD0"), b.sem("WD1")]
    WDg = [b.sem("WDg0"), b.sem("WDg1")]
    stgs_tots = [("STGS0", b.total["STGS0"]), ("STGS1", b.total["STGS1"])]
    wd_tok = {}
    wg_consumer = [None, None]
    wu_consumer = [None, None]
    sg_consumer = [None, None]
    hrow_cons = [[], []]

    for i in range(NIC):
        sl = i % 2
        t_wg = b.op("gpsimd",
                    lambda e, s, i=i, sl=sl: e.dma_start(
                        wgu_g[sl][:], w_gu_h[i, :, :, :]),
                    waits=[wg_consumer[sl]], dma=WG[sl])
        t_wu = b.op("gpsimd",
                    lambda e, s, i=i, sl=sl: e.dma_start(
                        wgu_u[sl][:], w_gu_h[NIC + i, :, :, :]),
                    waits=[wu_consumer[sl]], dma=WU[sl])
        h_ops = []
        bkg = list(range(NT4))
        mm_last = [None] * NT4
        for t4 in range(NT4):
            bk = bkg[t4]
            for k in range(NE):
                mm_last[t4] = b.op("tensor",
                                   lambda e, s, bk=bk, k=k, t4=t4, sl=sl: e.matmul(
                                       banks6[bk][:, 0:TT], wgu_g[sl][:, k, :],
                                       xn[:, k, t4 * TT:(t4 + 1) * TT],
                                       start=(k == 0), stop=(k == NE - 1)),
                                   waits=([t_wg, bank_free[bk]] if k == 0 else []),
                                   notok=(k < NE - 1))
        wg_consumer[sl] = mm_last[NT4 - 1]
        sg_evs = []
        for t4 in range(NT4):
            bk = bkg[t4]
            t_sg2 = b.op("scalar",
                         lambda e, s, bk=bk, t4=t4, sl=sl: e.activation(
                             sg_t[sl][:, t4 * TT:(t4 + 1) * TT], banks6[bk][:, 0:TT], A.Silu),
                         waits=[mm_last[t4], sg_consumer[sl]])
            sg_consumer[sl] = None
            bank_free[bk] = t_sg2
            sg_evs.append(t_sg2)
        bku = [4, 5, 0, 1][:NT4]
        mm_lastu = [None] * NT4
        for t4 in range(NT4):
            bk = bku[t4]
            for k in range(NE):
                mm_lastu[t4] = b.op("tensor",
                                    lambda e, s, bk=bk, k=k, t4=t4, sl=sl: e.matmul(
                                        banks6[bk][:, 0:TT], wgu_u[sl][:, k, :],
                                        xn[:, k, t4 * TT:(t4 + 1) * TT],
                                        start=(k == 0), stop=(k == NE - 1)),
                                    waits=([t_wu, bank_free[bk]] if k == 0 else []),
                                    notok=(k < NE - 1))
        wu_consumer[sl] = mm_lastu[NT4 - 1]
        for t4 in range(NT4):
            bk = bku[t4]
            t_h = b.op("vector",
                       lambda e, s, bk=bk, t4=t4, sl=sl: e.tensor_mul(
                           h_row[sl][:, t4 * TT:(t4 + 1) * TT],
                           sg_t[sl][:, t4 * TT:(t4 + 1) * TT], banks6[bk][:, 0:TT]),
                       waits=[mm_lastu[t4], sg_evs[t4]] + (hrow_cons[sl] if t4 == 0 else []))
            if t4 == 0:
                hrow_cons[sl] = []
            bank_free[bk] = t_h
            h_ops.append(t_h)
        sg_consumer[sl] = h_ops[-1]
        t_sh = b.op("sync", lambda e, s, i=i, sl=sl: e.dma_start(sc_h[i, :, :], h_row[sl][:]),
                    waits=h_ops + [("SH" + str(sl), b.total["SH" + str(sl)])], dma=SH[sl])
        hrow_cons[sl] = [t_sh]
        sh_tok_by_i[i] = t_sh
        if T == 2048 and i in (20, 36, 52):
            c = (20, 36, 52).index(i)
            hh0_pre_toks.append(b.op("sync",
                lambda e, s, c=c: e.dma_start(
                    hh[0][:, c * 16:(c + 1) * 16, :],
                    sc_h.rearrange("i p t -> p i t")[:, c * 16:(c + 1) * 16, 0:TQ]),
                waits=[sh_tok_by_i[c * 16 + 15]], dma="HH0"))
        if T == 2048 and i == NIC // 2:
            for part in range(4):
                wd_tok[0] = b.op("scalar",
                                 lambda e, s, part=part: e.dma_start(
                                     wdn_q[0][:, part * 16:(part + 1) * 16, :],
                                     sc_wdn[0, part, :, :, :]),
                                 waits=stgs_tots if part == 0 else [], dma=WD[0])

    p4a_end = b.snapshot()
    mm_p4a = ("MM", b.total["MM"])
    vv_p4a = ("VV", b.total["VV"])
    sc_p4a = ("SC", b.total["SC"])
    sh_tots = [("SH0", b.total["SH0"]), ("SH1", b.total["SH1"])]
    p4a_all = [mm_p4a, vv_p4a, sc_p4a] + sh_tots
    p4a.close()
    xn_stack.close()

    p4b = contextlib.ExitStack()

    def sb4b(name, shape, dtype=F32):
        sbufs[name] = p4b.enter_context(nc.sbuf_tensor(name, shape, dtype))
        return sbufs[name]

    wdn_q[1] = sb4b("wdn_q1", [128, NIC, 512], BF16)
    hh[1] = sb4b("hh1", [128, NIC, TQ], BF16)
    g_t[0] = sb4b("g_t0", [128, 4, TQ])
    g_t[1] = sb4b("g_t1", [128, 4, TQ])
    wdn_q[0] = sb4b("wdn_q0", [128, NIC, 512], BF16)
    hh[0] = sb4b("hh0", [128, NIC, TQ], BF16)

    HH = [b.sem("HH0"), b.sem("HH1")]
    SGSP = [b.sem("SGSP0"), b.sem("SGSP1")]
    hh_consumer = [[], []]
    gt_consumer = [[], []]

    def load_wdn(q, waits, eng="sync"):
        sem = WDg[q % 2] if eng == "gpsimd" else WD[q % 2]
        t_wd = None
        for part in range(4):
            t_wd = b.op(eng,
                        lambda e, s, q=q, part=part: e.dma_start(
                            wdn_q[q % 2][:, part * 16:(part + 1) * 16, :],
                            sc_wdn[q, part, :, :, :]),
                        waits=waits if part == 0 else [], dma=sem)
        return t_wd

    def load_hh(t8, waits):
        sl = t8 % 2
        return b.op("sync",
                    lambda e, s, t8=t8, sl=sl: e.dma_start(
                        hh[sl][:],
                        sc_h.rearrange("i p t -> p i t")[:, :, t8 * TQ:(t8 + 1) * TQ]),
                    waits=waits, dma=HH[sl])

    if T == 2048:
        hh_pre = {0: b.op("sync",
                          lambda e, s: e.dma_start(
                              hh[0][:, 48:64, :],
                              sc_h.rearrange("i p t -> p i t")[:, 48:64, 0:TQ]),
                          waits=sh_tots, dma="HH0")}
        if NT8 > 1:
            hh_pre[1] = load_hh(1, [mm_p4a, vv_p4a] + sh_tots)
        wd_tok[1] = load_wdn(1, [mm_p4a], eng="scalar")
    else:
        hh_pre = {0: load_hh(0, p4a_all)}
        if NT8 > 1:
            hh_pre[1] = load_hh(1, [])
        wd_tok[0] = load_wdn(0, p4a_all, eng="scalar")
        wd_tok[1] = load_wdn(1, [], eng="scalar")

    b.barrier(p4a_end, exclude=("CI",))

    t_cc = b.op("gpsimd",
                lambda e, s: e.collective_compute(
                    "AllGather", ALU.bypass,
                    replica_groups=[list(range(NCORES))],
                    ins=[cc_in[:, :]], outs=[cc_out[:, :, :]]),
                waits=[t_ci], inc=("CCS", 1))
    t_gs = b.op("gpsimd",
                lambda e, s: e.dma_start(gsb[:], cc_out.rearrange("r p c -> p r c")),
                waits=[t_cc], dma=GS)

    q2_mm = [None]
    for q in range(4):
        last_mm = None
        for t8 in range(NT8):
            sl = t8 % 2
            if q == 0 and t8 in hh_pre:
                t_hh = hh_pre[t8]
            else:
                t_hh = load_hh(t8, hh_consumer[sl])
            hh_consumer[sl] = []
            gops = []
            for hc in range(4):
                bk = mmcnt[0] % 4
                mmcnt[0] += 1
                t_mm = None
                for ic in range(NIC):
                    t_mm = b.op("tensor",
                                lambda e, s, bk=bk, ic=ic, hc=hc, sl=sl, q=q: e.matmul(
                                    pb[bk][:, 0:TQ], wdn_q[q % 2][:, ic, hc * 128:(hc + 1) * 128],
                                    hh[sl][:, ic, :],
                                    start=(ic == 0), stop=(ic == NIC - 1)),
                                waits=([wd_tok[q], t_hh, bank_free[bk]] if ic == 0 else []),
                                notok=(ic < NIC - 1))
                last_mm = t_mm
                t_gt = b.op("scalar",
                            lambda e, s, bk=bk, hc=hc, sl=sl: e.activation(
                                g_t[sl][:, hc, :], pb[bk][:, 0:TQ], A.Silu),
                            waits=[t_mm] + (gt_consumer[sl] if hc == 0 else []))
                if hc == 0:
                    gt_consumer[sl] = []
                bank_free[bk] = t_gt
                gops.append(t_gt)
                hh_consumer[sl].append(t_mm)
            t_sg3 = b.op("scalar",
                         lambda e, s, q=q, t8=t8, sl=sl: e.dma_start(
                             sc_g.rearrange("f p t -> p f t")[:, q * 4:(q + 1) * 4,
                                                              t8 * TQ:(t8 + 1) * TQ],
                             g_t[sl][:]),
                         waits=gops + [("SGSP" + str(sl), b.total["SGSP" + str(sl)])],
                         dma=SGSP[sl])
            gt_consumer[sl] = [t_sg3]
        if q + 2 <= 3:
            wd_tok[q + 2] = load_wdn(q + 2, [last_mm], eng="gpsimd")
        if q == 2:
            q2_mm[0] = last_mm

    p4b_end = b.snapshot()
    p4b.close()

    p5 = contextlib.ExitStack()

    def sb5(name, shape, dtype=F32):
        sbufs[name] = p5.enter_context(nc.sbuf_tensor(name, shape, dtype))
        return sbufs[name]

    wout = sb5("wout", [128, NE, D], BF16)
    WO2 = b.sem("WO2")
    wo_all = b.op("sync",
                  lambda e, s: e.dma_start(wout[:], sc_wout.rearrange("k p c -> p k c")),
                  waits=[("MM", p4b_end["MM"])], dma=WO2)

    b.barrier(p4b_end)

    d5 = sb5("d5", [128, NE, TQ])
    bp5 = sb5("bp5", [128, NE, TQ])
    ga5 = sb5("ga5", [128, NE, TQ])
    gg5 = sb5("gg5", [128, NE, TQ])
    st5 = sb5("st5", [128, NE, TQ])
    y5 = sb5("y5", [128, NE, TQ], BF16)
    s5 = sb5("s5", [128, NE, TQ])
    x5 = [sb5(f"x5_{i}", [128, D]) for i in range(2)]
    out5 = [sb5(f"out5{i}", [128, D]) for i in range(2)]

    gs_tok = ("GS", b.total["GS"])
    t_h0 = b.op("vector", lambda e, s: e.memset(hcomb[:], 0.0), waits=[gs_tok, l0_tok])
    prev = t_h0
    for jcore in range(NCORES - 1):
        t_a = b.op("vector",
                   lambda e, s, j=jcore: e.tensor_mul(tmp1[:], gsb[:, j, 0:NE], hcomb[:]),
                   waits=[prev])
        t_b2 = b.op("vector",
                    lambda e, s, j=jcore: e.tensor_add(tmp1[:], tmp1[:], gsb[:, j, NE:2 * NE]),
                    waits=[t_a])
        t_c = b.op("vector", lambda e, s: e.tensor_sub(tmp2[:], tmp1[:], hcomb[:]),
                   waits=[t_b2])
        t_d2 = b.op("vector",
                    lambda e, s, j=jcore: e.tensor_scalar_mul(tmp2[:], tmp2[:],
                                                              cmask[:, j:j + 1]),
                    waits=[t_c])
        prev = b.op("vector", lambda e, s: e.tensor_add(hcomb[:], hcomb[:], tmp2[:]),
                    waits=[t_d2])
    t_carry0 = b.op("vector", lambda e, s: e.tensor_copy(carry[:], hcomb[:]), waits=[prev])

    LD5 = b.sem("LD5")
    LB5 = b.sem("LB5")
    LG5 = b.sem("LG5")
    LGG = b.sem("LGG")
    LX5 = [b.sem("LX5a"), b.sem("LX5b")]
    SOUT = [b.sem("SOUT0"), b.sem("SOUT1")]
    y5_cons = []
    out5_cons = [[], []]
    carry_last = [t_carry0]

    def ld_sc(eng, sem, dst, src, t8, waits):
        return b.op(eng,
                    lambda e, s, t8=t8, dst=dst, src=src: e.dma_start(
                        dst[:], src.rearrange("f p t -> p f t")[:, :, t8 * TQ:(t8 + 1) * TQ]),
                    waits=waits, dma=sem)

    t_ld = ld_sc("sync", LD5, d5, sc_d, 0, [])
    t_lb = ld_sc("sync", LB5, bp5, sc_bp, 0, [])
    t_lg = ld_sc("gpsimd", LG5, ga5, sc_gate, 0, [])
    t_lgg = ld_sc("sync", LGG, gg5, sc_g, 0, [])

    t_x5 = [b.op("gpsimd",
                 lambda e, s, j2=j2: e.dma_start(x5[j2][:], x_h[j2 * 128:(j2 + 1) * 128, :]),
                 dma=LX5[j2]) for j2 in range(2)]

    def emit_scans(t8, t_ld, t_lb, first_extra=()):
        sc = []
        prev = carry_last[0]
        for fc in range(NE):
            t_s5 = b.op("vector",
                        lambda e, s, fc=fc, t8=t8: e.tensor_tensor_scan(
                            st5[:, fc, :], d5[:, fc, :], bp5[:, fc, :],
                            carry[:, fc:fc + 1], ALU.mult, ALU.subtract),
                        waits=([t_ld, t_lb, prev] + list(first_extra)) if fc == 0
                              else [sc[fc - 1]])
            sc.append(t_s5)
        t_cy = b.op("vector",
                    lambda e, s: e.tensor_copy(
                        carry[:, 0:NE], st5[:, :, TQ - 1:TQ].rearrange("p f o -> p (f o)")),
                    waits=[sc[-1]])
        carry_last[0] = t_cy
        return sc

    scans = emit_scans(0, t_ld, t_lb)
    pend = None
    for t8 in range(NT8):
        t_y = b.op("vector", lambda e, s: e.tensor_mul(y5[:], ga5[:], st5[:]),
                   waits=[t_lg] + scans + y5_cons)
        y5_cons = []
        if t8 + 1 < NT8:
            t_ldn = ld_sc("sync", LD5, d5, sc_d, t8 + 1, [scans[-1]])
            t_lbn = ld_sc("sync", LB5, bp5, sc_bp, t8 + 1, [scans[-1]])
            t_lgn = ld_sc("gpsimd", LG5, ga5, sc_gate, t8 + 1, [t_y])
            scans = emit_scans(t8 + 1, t_ldn, t_lbn, first_extra=[t_y])
        if pend is not None:
            pt8, pcps, px5 = pend
            adds = []
            for j2 in range(TQ // 128):
                row0 = pt8 * TQ + j2 * 128
                t_ad = b.op("vector",
                            lambda e, s, j2=j2: e.tensor_add(
                                out5[j2][:], out5[j2][:], x5[j2][:]),
                            waits=pcps[j2] + [px5[j2]])
                adds.append(t_ad)
                t_st = b.op("sync",
                            lambda e, s, row0=row0, j2=j2: e.dma_start(
                                out_h[row0:row0 + 128, :], out5[j2][:]),
                            waits=[t_ad], dma=SOUT[j2])
                out5_cons[j2] = [t_st]
            t_x5 = [b.op("gpsimd",
                         lambda e, s, j2=j2, t8=t8: e.dma_start(
                             x5[j2][:], x_h[t8 * TQ + j2 * 128:t8 * TQ + (j2 + 1) * 128, :]),
                         waits=[adds[j2]], dma=LX5[j2])
                    for j2 in range(2)]
        ev5 = []
        for e_ in range(NE):
            bk = e_ % 2
            t_mm = None
            for k in range(NE):
                t_mm = b.op("tensor",
                            lambda e, s, bk=bk, k=k, e_=e_: e.matmul(
                                pb[bk][:, 0:TQ], wout[:, k, e_ * 128:(e_ + 1) * 128],
                                y5[:, k, :], start=(k == 0), stop=(k == NE - 1)),
                            waits=([wo_all, t_y, bank_free[bk]]
                                   if k == 0 else []),
                            notok=(k < NE - 1))
            y5_cons = [t_mm]
            t_e5 = b.op("scalar",
                        lambda e, s, bk=bk, e_=e_: e.activation(
                            s5[:, e_, :], pb[bk][:, 0:TQ], A.Identity,
                            bias=bo_t[:, e_:e_ + 1]),
                        waits=[t_mm])
            bank_free[bk] = t_e5
            ev5.append(t_e5)
        t_rg = []
        for g4 in range(4):
            t_rg.append(b.op("vector",
                             lambda e, s, g4=g4: e.tensor_mul(
                                 s5[:, g4 * 4:(g4 + 1) * 4, :],
                                 s5[:, g4 * 4:(g4 + 1) * 4, :],
                                 gg5[:, g4 * 4:(g4 + 1) * 4, :]),
                             waits=[ev5[g4 * 4 + 3], t_lgg] if g4 == 0
                                   else [ev5[g4 * 4 + 3]]))
        if t8 + 1 < NT8:
            t_lggn = ld_sc("sync", LGG, gg5, sc_g, t8 + 1, [t_rg[-1]])
        cps = {0: [], 1: []}
        for j2 in range(TQ // 128):
            for grp in range(4):
                bk = (j2 * 4 + grp) % 2
                t_tr = None
                for qq in range(4):
                    fc = grp * 4 + qq
                    t_tr = b.op("tensor",
                                lambda e, s, bk=bk, qq=qq, fc=fc, j2=j2: e.transpose(
                                    banks6[4 + bk][:, qq * 128:(qq + 1) * 128],
                                    s5[:, fc, j2 * 128:(j2 + 1) * 128], eye32[:]),
                                waits=([t_rg[grp], bank_free[4 + bk], l0_tok]
                                       if qq == 0 else []),
                                notok=(qq < 3))
                t_cp = b.op("scalar",
                            lambda e, s, bk=bk, grp=grp, j2=j2: e.copy(
                                out5[j2][:, grp * 512:(grp + 1) * 512],
                                banks6[4 + bk][:, 0:512]),
                            waits=[t_tr] + (out5_cons[j2] if grp == 0 else []))
                if grp == 0:
                    out5_cons[j2] = []
                bank_free[4 + bk] = t_cp
                cps[j2].append(t_cp)
        pend = (t8, cps, t_x5)
        if t8 + 1 < NT8:
            t_ld, t_lb, t_lg, t_lgg = t_ldn, t_lbn, t_lgn, t_lggn

    pt8, pcps, px5 = pend
    for j2 in range(TQ // 128):
        row0 = pt8 * TQ + j2 * 128
        t_ad = b.op("vector",
                    lambda e, s, j2=j2: e.tensor_add(out5[j2][:], out5[j2][:], x5[j2][:]),
                    waits=pcps[j2] + [px5[j2]])
        t_st = b.op("sync",
                    lambda e, s, row0=row0, j2=j2: e.dma_start(
                        out_h[row0:row0 + 128, :], out5[j2][:]),
                    waits=[t_ad], dma=SOUT[j2])
        out5_cons[j2] = [t_st]

    b.wait("sync", [("SOUT0", b.total["SOUT0"]), ("SOUT1", b.total["SOUT1"])])

    sem_handles = {}
    for name in b.sem_names:
        sem_handles[name] = es.enter_context(nc.semaphore(name))
    with nc.Block() as blk:
        @blk.tensor
        def _(e):
            b.replay("tensor", e, sem_handles)

        @blk.scalar
        def _(e):
            b.replay("scalar", e, sem_handles)

        @blk.vector
        def _(e):
            b.replay("vector", e, sem_handles)

        @blk.gpsimd
        def _(e):
            b.replay("gpsimd", e, sem_handles)

        @blk.sync
        def _(e):
            b.replay("sync", e, sem_handles)
    p5.close()
    es.close()
    return nc


_NC_CACHE = {}


def _get_nc(T):
    if T not in _NC_CACHE:
        _NC_CACHE[T] = build_nc(T)
    return _NC_CACHE[T]


def _bias_tab(v):
    return np.ascontiguousarray(np.asarray(v, dtype=np.float32).reshape(NE, 128).T)


def _attn_blocks(w):
    w = np.asarray(w, dtype=np.float32)
    return np.ascontiguousarray(w.reshape(NE, 128, NE, 128).transpose(2, 1, 0, 3))


def kernel(L0_attn_x, L0_attn_W_in, L0_attn_b_in, L0_attn_W_decay, L0_attn_b_decay,
           L0_attn_W_gate, L0_attn_b_gate, L0_attn_W_out, L0_attn_b_out,
           L0_attn_decay_bias, L0_ffn_w_gate_up, L0_ffn_w_down):
    T = SEQ // NCORES
    x = np.ascontiguousarray(np.asarray(L0_attn_x, dtype=np.float32).reshape(SEQ, D))
    w_gu = np.asarray(L0_ffn_w_gate_up, dtype=np.float32)
    w_gu_r = np.ascontiguousarray(
        w_gu.reshape(2 * NIC, 128, NE, 128).transpose(0, 3, 2, 1))
    w_dn = np.asarray(L0_ffn_w_down, dtype=np.float32)
    w_dn_r = np.ascontiguousarray(
        w_dn.T.reshape(4, 16, 128, 4, 512).transpose(3, 0, 2, 1, 4))
    w_out = np.ascontiguousarray(np.asarray(L0_attn_W_out, dtype=np.float32))
    base = dict(
        w_in_r=_attn_blocks(L0_attn_W_in), w_dec_r=_attn_blocks(L0_attn_W_decay),
        w_gate_r=_attn_blocks(L0_attn_W_gate), w_out=w_out,
        w_gu_r=w_gu_r, w_dn_r=w_dn_r,
        b_in_t=_bias_tab(L0_attn_b_in), b_dec_t=_bias_tab(L0_attn_b_decay),
        b_db_t=_bias_tab(L0_attn_decay_bias), b_gate_t=_bias_tab(L0_attn_b_gate),
        b_out_t=_bias_tab(L0_attn_b_out),
        eye32=np.eye(128, dtype=np.float32),
    )
    in_maps = []
    for c in range(NCORES):
        m = dict(base)
        m["x"] = np.ascontiguousarray(x[c * T:(c + 1) * T, :])
        cm = np.zeros((128, NCORES), dtype=np.float32)
        cm[:, :c] = 1.0
        m["cmask"] = cm
        in_maps.append(m)
    nc = _get_nc(T)
    res = run_bass_kernel_spmd(nc, in_maps, core_ids=list(range(NCORES)))
    out = np.concatenate([np.asarray(r["out"]) for r in res.results], axis=0)
    return out.reshape(1, SEQ, D).astype(np.float32)


# version 15
# speedup vs baseline: 1.0154x; 1.0069x over previous
import os
os.environ.setdefault("BASS_INTERP_NUM_WORKERS", "8")

import contextlib
import numpy as np

import concourse.bass as bass
import concourse.mybir as mybir
from concourse.bass_utils import run_bass_kernel_spmd

F32 = mybir.dt.float32
BF16 = mybir.dt.bfloat16
A = mybir.ActivationFunctionType
ALU = mybir.AluOpType
AX = mybir.AxisListType

D = 2048
NE = D // 128
INTER = 8192
NIC = INTER // 128
SEQ = 16384
NCORES = 8
EPS = 1e-5


class KB:

    ENGS = ("tensor", "scalar", "vector", "gpsimd", "sync")

    def __init__(self):
        self.recs = {e: [] for e in self.ENGS}
        self.sem_names = []
        self.total = {}
        self.selfsem = {"tensor": "MM", "scalar": "SC", "vector": "VV"}
        for s in ("MM", "SC", "VV"):
            self._reg(s)

    def _reg(self, name):
        if name not in self.total:
            self.sem_names.append(name)
            self.total[name] = 0

    def sem(self, name):
        self._reg(name)
        return name

    def op(self, eng, fn, waits=(), dma=None, inc=None, notok=False):
        waits = [w for w in waits if w is not None]
        if notok:
            self.recs[eng].append((fn, waits, None))
            return None
        if dma is not None:
            self._reg(dma)
            self.total[dma] += 16
            tok = (dma, self.total[dma])
            self.recs[eng].append((fn, waits, (dma, 16)))
            return tok
        if inc is not None:
            s, n = inc
            self._reg(s)
            self.total[s] += n
            tok = (s, self.total[s])
            self.recs[eng].append((fn, waits, (s, n)))
            return tok
        if eng in self.selfsem:
            s = self.selfsem[eng]
            self.total[s] += 1
            tok = (s, self.total[s])
            self.recs[eng].append((fn, waits, (s, 1)))
            return tok
        self.recs[eng].append((fn, waits, None))
        return None

    def wait(self, eng, toks):
        toks = [t for t in toks if t is not None]
        if toks:
            self.recs[eng].append((None, toks, None))

    def snapshot(self):
        return dict(self.total)

    def barrier(self, snap=None, exclude=()):
        snap = snap or self.snapshot()
        toks = [(s, v) for s, v in snap.items() if v > 0 and s not in exclude]
        for e in self.ENGS:
            self.wait(e, toks)

    def replay(self, eng_name, eng, sems):
        waited = {}
        for fn, waits, inc in self.recs[eng_name]:
            for (s, v) in waits:
                if v > 0 and waited.get(s, -1) < v:
                    eng.wait_ge(sems[s], v)
                    waited[s] = v
            if fn is not None:
                instr = fn(eng, sems)
                if inc is not None and instr is not None:
                    instr.then_inc(sems[inc[0]], inc[1])


def build_nc(T):
    NJ = T // 128
    TT = min(512, T)
    NT4 = T // TT
    TQ = min(256, T)
    NT8 = T // TQ

    nc = bass.Bass("TRN2", target_bir_lowering=False)

    x_h = nc.dram_tensor("x", [T, D], F32, kind="ExternalInput").ap()
    w_in_h = nc.dram_tensor("w_in_r", [NE, 128, NE, 128], F32, kind="ExternalInput").ap()
    w_dec_h = nc.dram_tensor("w_dec_r", [NE, 128, NE, 128], F32, kind="ExternalInput").ap()
    w_gate_h = nc.dram_tensor("w_gate_r", [NE, 128, NE, 128], F32, kind="ExternalInput").ap()
    w_out_h = nc.dram_tensor("w_out", [D, D], F32, kind="ExternalInput").ap()
    w_gu_h = nc.dram_tensor("w_gu_r", [2 * NIC, 128, NE, 128], F32, kind="ExternalInput").ap()
    w_dn_h = nc.dram_tensor("w_dn_r", [4, 4, 128, 16, 512], F32, kind="ExternalInput").ap()
    b_in_h = nc.dram_tensor("b_in_t", [128, NE], F32, kind="ExternalInput").ap()
    b_dec_h = nc.dram_tensor("b_dec_t", [128, NE], F32, kind="ExternalInput").ap()
    b_db_h = nc.dram_tensor("b_db_t", [128, NE], F32, kind="ExternalInput").ap()
    b_gate_h = nc.dram_tensor("b_gate_t", [128, NE], F32, kind="ExternalInput").ap()
    b_out_h = nc.dram_tensor("b_out_t", [128, NE], F32, kind="ExternalInput").ap()
    eye_h = nc.dram_tensor("eye32", [128, 128], F32, kind="ExternalInput").ap()
    mask_h = nc.dram_tensor("cmask", [128, NCORES], F32, kind="ExternalInput").ap()
    out_h = nc.dram_tensor("out", [T, D], F32, kind="ExternalOutput").ap()

    sc_d = nc.dram_tensor("sc_d", [NE, 128, T], F32, kind="Internal").ap()
    sc_bp = nc.dram_tensor("sc_bp", [NE, 128, T], F32, kind="Internal").ap()
    sc_gate = nc.dram_tensor("sc_gate", [NE, 128, T], F32, kind="Internal").ap()
    sc_g = nc.dram_tensor("sc_g", [NE, 128, T], F32, kind="Internal").ap()
    sc_h = nc.dram_tensor("sc_h", [NIC, 128, T], BF16, kind="Internal").ap()
    cc_in = nc.dram_tensor("cc_in", [128, 2 * NE], F32, kind="Internal").ap()
    cc_out = nc.dram_tensor("cc_out", [NCORES, 128, 2 * NE], F32, kind="Internal").ap()
    sc_wdn = nc.dram_tensor("sc_wdn", [4, 4, 128, 16, 512], BF16, kind="Internal").ap()
    sc_wout = nc.dram_tensor("sc_wout", [NE, 128, D], BF16, kind="Internal").ap()

    b = KB()
    es = contextlib.ExitStack()
    sbufs = {}

    def sb(name, shape, dtype=F32):
        sbufs[name] = es.enter_context(nc.sbuf_tensor(name, shape, dtype))
        return sbufs[name]

    bi_t = sb("bi_t", [128, NE])
    bdc_t = sb("bdc_t", [128, NE])
    bdb_t = sb("bdb_t", [128, NE])
    bg_t = sb("bg_t", [128, NE])
    bo_t = sb("bo_t", [128, NE])
    eye32 = sb("eye32s", [128, 128])
    eye16 = sb("eye16s", [128, 128], BF16)
    cmask = sb("cmasks", [128, NCORES])
    pair = sb("pair", [128, 2 * NE])
    gsb = sb("gsb", [128, NCORES, 2 * NE])
    hcomb = sb("hcomb", [128, NE])
    tmp1 = sb("tmp1", [128, NE])
    tmp2 = sb("tmp2", [128, NE])
    carry = sb("carry", [128, NE])

    pb = [es.enter_context(nc.psum_tensor(f"pb{i}", [128, 512], F32)) for i in range(4)]
    pb16 = [es.enter_context(nc.psum_tensor(f"pb16_{i}", [128, 1024], BF16)) for i in range(2)]
    pb32 = [es.enter_context(nc.psum_tensor(f"pb32_{i}", [128, 512], F32)) for i in range(2)]
    banks6 = pb + pb32
    bank_free = [None] * 6
    tb16_free = [None] * 2
    ring = [0]

    def ring_next():
        bk = ring[0] % 6
        ring[0] += 1
        return bk

    xn_stack = contextlib.ExitStack()
    xn = xn_stack.enter_context(nc.sbuf_tensor("xn", [128, NE, T], BF16))

    L0 = b.sem("L0")
    for dst, src in ((bi_t, b_in_h), (bdc_t, b_dec_h), (bdb_t, b_db_h),
                     (bg_t, b_gate_h), (bo_t, b_out_h), (eye32, eye_h),
                     (cmask, mask_h)):
        b.op("sync", lambda e, s, dst=dst, src=src: e.dma_start(dst[:], src[:, :]),
             dma="L0")
    l0_tok = ("L0", b.total["L0"])
    t_eye16 = b.op("scalar", lambda e, s: e.copy(eye16[:], eye32[:]), waits=[l0_tok])
    t_bdc = b.op("vector", lambda e, s: e.tensor_add(bdc_t[:], bdc_t[:], bdb_t[:]),
                 waits=[l0_tok])

    p1 = contextlib.ExitStack()

    def sb1(name, shape, dtype=F32):
        sbufs[name] = p1.enter_context(nc.sbuf_tensor(name, shape, dtype))
        return sbufs[name]

    NSL = 4
    x_t = [sb1(f"x_t{i}", [128, D]) for i in range(NSL)]
    xn_tm = [sb1(f"xn_tm{i}", [128, D], BF16) for i in range(NSL)]
    st2 = [{k: sb1(f"st{i}_" + k, [128, 1]) for k in
            ("sx", "sq", "mean", "msq", "m2", "vv1", "vp", "s1", "r0",
             "aa", "bb", "cc", "rr", "nb")} for i in range(NSL)]
    xsqs = [sb1(f"xsqs{i}", [128, D]) for i in range(NSL)]

    STGL = [b.sem("STGL0"), b.sem("STGL1")]
    STGS = [b.sem("STGS0"), b.sem("STGS1")]
    precast_jobs = []
    for q in range(4):
        for part in range(4):
            precast_jobs.append((
                lambda sl, q=q, part=part: (stg_w[sl][:, :, :], w_dn_h[q, part, :, :, :]),
                lambda sl, q=q, part=part: (sc_wdn[q, part, :, :, :], stg_w[sl][:, :, :])))
    for k in range(NE):
        precast_jobs.append((
            lambda sl, k=k: (stg_w[sl][:, 0:4, :].rearrange("p a c -> p (a c)"),
                             w_out_h[k * 128:(k + 1) * 128, :]),
            lambda sl, k=k: (sc_wout[k, :, :],
                             stg_w[sl][:, 0:4, :].rearrange("p a c -> p (a c)"))))
    pc_i = [0]
    pc_store_tok = [None, None]
    xn_t4_tok = {}

    def emit_precast(njobs):
        for _ in range(njobs):
            i = pc_i[0]
            if i >= len(precast_jobs): return
            pc_i[0] += 1
            sl = i % 2
            ldfn, stfn = precast_jobs[i]
            t_l = b.op("gpsimd",
                       lambda e, s, ldfn=ldfn, sl=sl: e.dma_start(*ldfn(sl)),
                       waits=[pc_store_tok[sl]], dma=STGL[sl])
            pc_store_tok[sl] = b.op("sync",
                                    lambda e, s, stfn=stfn, sl=sl: e.dma_start(*stfn(sl)),
                                    waits=[t_l], dma=STGS[sl])

    LX = [b.sem(f"LX{i}") for i in range(NSL)]
    xslot_consumers = [[] for _ in range(NSL)]
    xnslot_consumers = [[] for _ in range(NSL)]
    xsq_consumer = [None] * NSL
    prev_stats = [None] * NSL

    for j in range(NJ):
        sl = j % NSL
        t_load = b.op(
            "sync",
            lambda e, s, sl=sl, j=j: e.dma_start(x_t[sl][:], x_h[j * 128:(j + 1) * 128, :]),
            waits=xslot_consumers[sl], dma=LX[sl])
        xslot_consumers[sl] = []

        st_ = st2[sl]
        myxsq = xsqs[sl]
        t_rs1 = b.op("scalar",
                     lambda e, s, sl=sl, myxsq=myxsq, st_=st_: e.activation(
                         myxsq[:], x_t[sl][:], A.Identity, accum_out=st_["sx"][:]),
                     waits=[t_load, prev_stats[sl], xsq_consumer[sl]])
        t_rs2 = b.op("scalar",
                     lambda e, s, sl=sl, myxsq=myxsq, st_=st_: e.activation(
                         myxsq[:], x_t[sl][:], A.Square, accum_out=st_["sq"][:]),
                     waits=[t_rs1])
        t_sq = t_rs2
        xsq_consumer[sl] = t_rs2
        inv = 1.0 / D
        t_m = b.op("vector", lambda e, s, st_=st_: e.tensor_scalar_mul(st_["mean"][:], st_["sx"][:], inv),
                   waits=[t_rs1])
        t_mq = b.op("vector", lambda e, s, st_=st_: e.tensor_scalar_mul(st_["msq"][:], st_["sq"][:], inv),
                    waits=[t_rs2])
        t_m2 = b.op("vector", lambda e, s, st_=st_: e.tensor_mul(st_["m2"][:], st_["mean"][:], st_["mean"][:]),
                    waits=[t_m])
        t_v1 = b.op("vector", lambda e, s, st_=st_: e.tensor_sub(st_["vv1"][:], st_["msq"][:], st_["m2"][:]),
                    waits=[t_mq, t_m2])
        t_vp = b.op("vector", lambda e, s, st_=st_: e.tensor_scalar_add(st_["vp"][:], st_["vv1"][:], EPS),
                    waits=[t_v1])
        t_s1 = b.op("scalar", lambda e, s, st_=st_: e.activation(st_["s1"][:], st_["vp"][:], A.Sqrt),
                    waits=[t_vp])
        t_r0 = b.op("vector", lambda e, s, st_=st_: e.reciprocal(st_["r0"][:], st_["s1"][:]),
                    waits=[t_s1])
        t_aa = b.op("vector", lambda e, s, st_=st_: e.tensor_mul(st_["aa"][:], st_["r0"][:], st_["r0"][:]),
                    waits=[t_r0])
        t_bb = b.op("vector", lambda e, s, st_=st_: e.tensor_mul(st_["bb"][:], st_["aa"][:], st_["vp"][:]),
                    waits=[t_aa, t_vp])
        t_cc = b.op("vector", lambda e, s, st_=st_: e.tensor_scalar(st_["cc"][:], st_["bb"][:], -0.5, 1.5,
                                                           ALU.mult, ALU.add),
                    waits=[t_bb])
        t_rr = b.op("vector", lambda e, s, st_=st_: e.tensor_mul(st_["rr"][:], st_["r0"][:], st_["cc"][:]),
                    waits=[t_cc])
        t_nb = b.op("vector", lambda e, s, st_=st_: e.tensor_scalar(st_["nb"][:], st_["mean"][:],
                                                           st_["rr"][:, 0:1], -1.0,
                                                           ALU.mult, ALU.mult),
                    waits=[t_rr])
        t_xn = b.op("scalar",
                    lambda e, s, sl=sl, st_=st_: e.activation(xn_tm[sl][:], x_t[sl][:], A.Identity,
                                                              bias=st_["nb"][:, 0:1],
                                                              scale=st_["rr"][:, 0:1]),
                    waits=[t_nb, xnslot_consumers[sl][-1] if xnslot_consumers[sl] else None])
        prev_stats[sl] = t_xn
        xnslot_consumers[sl] = []
        xslot_consumers[sl] = [t_sq, t_rs1, t_xn]
        for grp in range(4):
            bk = (j * 4 + grp) % 2
            t_tr = None
            for q in range(4):
                fc = grp * 4 + q
                t_tr = b.op(
                    "tensor",
                    lambda e, s, bk=bk, q=q, fc=fc, sl=sl: e.transpose(
                        pb16[bk][:, q * 128:(q + 1) * 128],
                        xn_tm[sl][:, fc * 128:(fc + 1) * 128], eye16[:]),
                    waits=([t_xn, t_eye16, tb16_free[bk]] if q == 0 else []),
                    notok=(q < 3))
            t_ev = b.op(
                "scalar",
                lambda e, s, bk=bk, grp=grp, j=j: e.copy(
                    xn[:, grp * 4:(grp + 1) * 4, j * 128:(j + 1) * 128],
                    pb16[bk][:, 0:512].rearrange("p (a c) -> p a c", a=4)),
                waits=[t_tr])
            tb16_free[bk] = t_ev
            xnslot_consumers[sl].append(t_ev)
            xn_t4_tok[j // 4] = t_ev

    b.barrier()
    p1.close()
    p2 = contextlib.ExitStack()

    def sb2(name, shape, dtype=F32):
        sbufs[name] = p2.enter_context(nc.sbuf_tensor(name, shape, dtype))
        return sbufs[name]

    stg_w = [sb2(f"stg_w{i}", [128, NE, 512], BF16) for i in range(2)]
    wcol = [sb2(f"wcol{i}", [128, NE, 128], BF16) for i in range(2)]
    winp = sb2("winp", [128, T])
    sgrow = sb2("sgrow", [128, T])
    irow = sb2("irow", [128, T])
    drow = [sb2(f"drow{i}", [128, T]) for i in range(2)]
    bprow = [sb2(f"bprow{i}", [128, T]) for i in range(2)]
    gaterow = [sb2(f"gaterow{i}", [128, T]) for i in range(2)]
    strash = sb2("strash", [128, T])
    prow = sb2("prow", [128, T])

    WC = [b.sem("WC0"), b.sem("WC1")]
    SD = [b.sem("SD0"), b.sem("SD1")]
    SBP = [b.sem("SBP0"), b.sem("SBP1")]
    SGA = [b.sem("SGA0"), b.sem("SGA1")]
    wcol_consumer = [None, None]
    row_consumers = {"winp": [], "sg": [], "irow": [], "strash": [], "prow": []}
    drow_cons = [[], []]
    bprow_cons = [[], []]
    gaterow_cons = [[], []]
    mmcnt = [0]

    wsrc = {0: w_in_h, 1: w_dec_h, 2: w_gate_h}
    use = 0
    for e_ in range(NE):
        emit_precast(2 if NE >= 16 else 32 // NE)
        evacs = {}
        for widx in range(3):
            slw = use % 2
            t_w = b.op(
                "gpsimd",
                lambda e, s, widx=widx, e_=e_, slw=slw: e.dma_start(
                    wcol[slw][:], wsrc[widx][e_, :, :, :]),
                waits=[wcol_consumer[slw]], dma=WC[slw])
            use += 1
            evs = []
            bkmap = list(range(NT4))
            mm_last = [None] * NT4
            for t4 in range(NT4):
                bk = bkmap[t4]
                for k in range(NE):
                    mm_last[t4] = b.op(
                        "tensor",
                        lambda e, s, bk=bk, k=k, t4=t4, slw=slw: e.matmul(
                            banks6[bk][:, 0:TT], wcol[slw][:, k, :],
                            xn[:, k, t4 * TT:(t4 + 1) * TT],
                            start=(k == 0), stop=(k == NE - 1)),
                        waits=([t_w, bank_free[bk]] if k == 0 else []),
                        notok=(k < NE - 1))
            for t4 in range(NT4):
                bk = bkmap[t4]
                t_mm = mm_last[t4]
                if widx == 0:
                    t_e1 = b.op("scalar",
                                lambda e, s, bk=bk, t4=t4, e_=e_: e.activation(
                                    winp[:, t4 * TT:(t4 + 1) * TT], banks6[bk][:, 0:TT],
                                    A.Identity, bias=bi_t[:, e_:e_ + 1]),
                                waits=[t_mm] + row_consumers["winp"])
                    t_e2 = b.op("scalar",
                                lambda e, s, bk=bk, t4=t4, e_=e_: e.activation(
                                    sgrow[:, t4 * TT:(t4 + 1) * TT], banks6[bk][:, 0:TT],
                                    A.Sigmoid, bias=bi_t[:, e_:e_ + 1]),
                                waits=row_consumers["sg"])
                    bank_free[bk] = t_e2
                    evs += [t_e1, t_e2]
                elif widx == 1:
                    sld = e_ % 2
                    t_e1 = b.op("scalar",
                                lambda e, s, bk=bk, t4=t4, e_=e_, sld=sld: e.activation(
                                    drow[sld][:, t4 * TT:(t4 + 1) * TT], banks6[bk][:, 0:TT],
                                    A.Sigmoid, bias=bdc_t[:, e_:e_ + 1]),
                                waits=[t_mm, t_bdc] + drow_cons[sld] +
                                      [("SD" + str(sld), b.total["SD" + str(sld)])])
                    if t4 == 0:
                        drow_cons[sld] = []
                    bank_free[bk] = t_e1
                    evs.append(t_e1)
                else:
                    sld = e_ % 2
                    t_e1 = b.op("scalar",
                                lambda e, s, bk=bk, t4=t4, e_=e_, sld=sld: e.activation(
                                    gaterow[sld][:, t4 * TT:(t4 + 1) * TT], banks6[bk][:, 0:TT],
                                    A.Sigmoid, bias=bg_t[:, e_:e_ + 1]),
                                waits=[t_mm] + gaterow_cons[sld] +
                                      [("SGA" + str(sld), b.total["SGA" + str(sld)])])
                    if t4 == 0:
                        gaterow_cons[sld] = []
                    bank_free[bk] = t_e1
                    evs.append(t_e1)
            wcol_consumer[slw] = mm_last[NT4 - 1]
            evacs[widx] = evs

        sld = e_ % 2
        t_i = b.op("vector", lambda e, s: e.tensor_mul(irow[:], winp[:], sgrow[:]),
                   waits=evacs[0] + row_consumers["irow"])
        row_consumers["winp"] = [t_i]
        row_consumers["sg"] = [t_i]
        t_bp = b.op("vector",
                    lambda e, s, sld=sld: e.scalar_tensor_tensor(
                        bprow[sld][:], drow[sld][:], 1.0, irow[:], ALU.subtract, ALU.mult),
                    waits=evacs[1] + [t_i] + bprow_cons[sld] +
                          [("SBP" + str(sld), b.total["SBP" + str(sld)])])
        bprow_cons[sld] = []
        row_consumers["irow"] = [t_bp]
        t_sc = b.op("vector",
                    lambda e, s, sld=sld: e.tensor_tensor_scan(
                        strash[:], drow[sld][:], bprow[sld][:], 0.0, ALU.mult, ALU.subtract),
                    waits=[t_bp] + row_consumers["strash"])
        t_cs = b.op("vector",
                    lambda e, s, e_=e_: e.tensor_copy(pair[:, NE + e_:NE + e_ + 1],
                                                      strash[:, T - 1:T]),
                    waits=[t_sc])
        row_consumers["strash"] = [t_cs]
        t_ps = b.op("vector",
                    lambda e, s, sld=sld: e.tensor_tensor_scan(
                        prow[:], drow[sld][:], drow[sld][:], 1.0, ALU.mult, ALU.bypass),
                    waits=[t_sc] + row_consumers["prow"])
        t_cp = b.op("vector",
                    lambda e, s, e_=e_: e.tensor_copy(pair[:, e_:e_ + 1], prow[:, T - 1:T]),
                    waits=[t_ps])
        row_consumers["prow"] = [t_cp]
        t_sd = b.op("sync", lambda e, s, sld=sld, e_=e_: e.dma_start(sc_d[e_, :, :], drow[sld][:]),
                    waits=[t_ps], dma=SD[sld])
        t_sb_ = b.op("sync", lambda e, s, sld=sld, e_=e_: e.dma_start(sc_bp[e_, :, :], bprow[sld][:]),
                     waits=[t_sc], dma=SBP[sld])
        t_sg_ = b.op("sync", lambda e, s, sld=sld, e_=e_: e.dma_start(sc_gate[e_, :, :], gaterow[sld][:]),
                     waits=evacs[2], dma=SGA[sld])
        drow_cons[sld] = [t_ps]
        bprow_cons[sld] = [t_sc]
        gaterow_cons[sld] = []

    CI = b.sem("CI")
    CCS = b.sem("CCS")
    GS = b.sem("GS")
    pair_done = ("VV", b.total["VV"])
    t_ci = b.op("sync", lambda e, s: e.dma_start(cc_in[:, :], pair[:]),
                waits=[pair_done], dma=CI)

    p2_end = b.snapshot()
    b.barrier(p2_end, exclude=("CI",))
    p2.close()

    p4a = contextlib.ExitStack()

    def sb4(name, shape, dtype=F32):
        sbufs[name] = p4a.enter_context(nc.sbuf_tensor(name, shape, dtype))
        return sbufs[name]

    wgu_g = [sb4(f"wgu_g{i}", [128, NE, 128], BF16) for i in range(2)]
    wgu_u = [sb4(f"wgu_u{i}", [128, NE, 128], BF16) for i in range(2)]
    sg_t = [sb4(f"sg_t{i}", [128, T]) for i in range(2)]
    h_row = [sb4(f"h_row{i}", [128, T], BF16) for i in range(2)]

    WG = [b.sem("WG0"), b.sem("WG1")]
    WU = [b.sem("WU0"), b.sem("WU1")]
    SH = [b.sem("SH0"), b.sem("SH1")]
    wdn_q = [None, None]
    hh = [None, None]
    g_t = [None, None]
    sh_tok_by_i = {}
    hh0_pre_toks = []
    hh_sched = set()
    hh_tok = {}
    WD = [b.sem("WD0"), b.sem("WD1")]
    WDg = [b.sem("WDg0"), b.sem("WDg1")]
    stgs_tots = [("STGS0", b.total["STGS0"]), ("STGS1", b.total["STGS1"])]
    wd_tok = {}
    wg_consumer = [None, None]
    wu_consumer = [None, None]
    sg_consumer = [None, None]
    hrow_cons = [[], []]

    for i in range(NIC):
        sl = i % 2
        t_wg = b.op("gpsimd",
                    lambda e, s, i=i, sl=sl: e.dma_start(
                        wgu_g[sl][:], w_gu_h[i, :, :, :]),
                    waits=[wg_consumer[sl]], dma=WG[sl])
        t_wu = b.op("gpsimd",
                    lambda e, s, i=i, sl=sl: e.dma_start(
                        wgu_u[sl][:], w_gu_h[NIC + i, :, :, :]),
                    waits=[wu_consumer[sl]], dma=WU[sl])
        h_ops = []
        bkg = list(range(NT4))
        mm_last = [None] * NT4
        for t4 in range(NT4):
            bk = bkg[t4]
            for k in range(NE):
                mm_last[t4] = b.op("tensor",
                                   lambda e, s, bk=bk, k=k, t4=t4, sl=sl: e.matmul(
                                       banks6[bk][:, 0:TT], wgu_g[sl][:, k, :],
                                       xn[:, k, t4 * TT:(t4 + 1) * TT],
                                       start=(k == 0), stop=(k == NE - 1)),
                                   waits=([t_wg, bank_free[bk]] if k == 0 else []),
                                   notok=(k < NE - 1))
        wg_consumer[sl] = mm_last[NT4 - 1]
        sg_evs = []
        for t4 in range(NT4):
            bk = bkg[t4]
            t_sg2 = b.op("scalar",
                         lambda e, s, bk=bk, t4=t4, sl=sl: e.activation(
                             sg_t[sl][:, t4 * TT:(t4 + 1) * TT], banks6[bk][:, 0:TT], A.Silu),
                         waits=[mm_last[t4], sg_consumer[sl]])
            sg_consumer[sl] = None
            bank_free[bk] = t_sg2
            sg_evs.append(t_sg2)
        bku = [4, 5, 0, 1][:NT4]
        mm_lastu = [None] * NT4
        for t4 in range(NT4):
            bk = bku[t4]
            for k in range(NE):
                mm_lastu[t4] = b.op("tensor",
                                    lambda e, s, bk=bk, k=k, t4=t4, sl=sl: e.matmul(
                                        banks6[bk][:, 0:TT], wgu_u[sl][:, k, :],
                                        xn[:, k, t4 * TT:(t4 + 1) * TT],
                                        start=(k == 0), stop=(k == NE - 1)),
                                    waits=([t_wu, bank_free[bk]] if k == 0 else []),
                                    notok=(k < NE - 1))
        wu_consumer[sl] = mm_lastu[NT4 - 1]
        for t4 in range(NT4):
            bk = bku[t4]
            t_h = b.op("vector",
                       lambda e, s, bk=bk, t4=t4, sl=sl: e.tensor_mul(
                           h_row[sl][:, t4 * TT:(t4 + 1) * TT],
                           sg_t[sl][:, t4 * TT:(t4 + 1) * TT], banks6[bk][:, 0:TT]),
                       waits=[mm_lastu[t4], sg_evs[t4]] + (hrow_cons[sl] if t4 == 0 else []))
            if t4 == 0:
                hrow_cons[sl] = []
            bank_free[bk] = t_h
            h_ops.append(t_h)
        sg_consumer[sl] = h_ops[-1]
        t_sh = b.op("sync", lambda e, s, i=i, sl=sl: e.dma_start(sc_h[i, :, :], h_row[sl][:]),
                    waits=h_ops + [("SH" + str(sl), b.total["SH" + str(sl)])], dma=SH[sl])
        hrow_cons[sl] = [t_sh]
        sh_tok_by_i[i] = t_sh
        if T == 2048 and i in (20, 36, 52):
            c = (20, 36, 52).index(i)
            hh0_pre_toks.append(b.op("sync",
                lambda e, s, c=c: e.dma_start(
                    hh[0][:, c * 16:(c + 1) * 16, :],
                    sc_h.rearrange("i p t -> p i t")[:, c * 16:(c + 1) * 16, 0:TQ]),
                waits=[sh_tok_by_i[c * 16 + 15]], dma="HH0"))
        if T == 2048 and i == NIC // 2:
            for part in range(4):
                wd_tok[0] = b.op("scalar",
                                 lambda e, s, part=part: e.dma_start(
                                     wdn_q[0][:, part * 16:(part + 1) * 16, :],
                                     sc_wdn[0, part, :, :, :]),
                                 waits=stgs_tots if part == 0 else [], dma=WD[0])

    p4a_end = b.snapshot()
    mm_p4a = ("MM", b.total["MM"])
    vv_p4a = ("VV", b.total["VV"])
    sc_p4a = ("SC", b.total["SC"])
    sh_tots = [("SH0", b.total["SH0"]), ("SH1", b.total["SH1"])]
    p4a_all = [mm_p4a, vv_p4a, sc_p4a] + sh_tots
    p4a.close()
    xn_stack.close()

    p4b = contextlib.ExitStack()

    def sb4b(name, shape, dtype=F32):
        sbufs[name] = p4b.enter_context(nc.sbuf_tensor(name, shape, dtype))
        return sbufs[name]

    wdn_q[1] = sb4b("wdn_q1", [128, NIC, 512], BF16)
    hh[1] = sb4b("hh1", [128, NIC, TQ], BF16)
    for _i in range(2):
        g_t[_i] = sb4b(f"g_t{_i}", [128, 4, TQ])
    wdn_q[0] = sb4b("wdn_q0", [128, NIC, 512], BF16)
    hh[0] = sb4b("hh0", [128, NIC, TQ], BF16)

    HH = [b.sem("HH0"), b.sem("HH1")]
    SGSP = [b.sem(f"SGSP{i}") for i in range(2)]
    hh_consumer = [[], []]
    gt_consumer = [[], []]

    def load_wdn(q, waits, eng="sync"):
        sem = WDg[q % 2] if eng == "gpsimd" else WD[q % 2]
        t_wd = None
        for part in range(4):
            t_wd = b.op(eng,
                        lambda e, s, q=q, part=part: e.dma_start(
                            wdn_q[q % 2][:, part * 16:(part + 1) * 16, :],
                            sc_wdn[q, part, :, :, :]),
                        waits=waits if part == 0 else [], dma=sem)
        return t_wd

    def load_hh(t8, waits):
        sl = t8 % 2
        return b.op("sync",
                    lambda e, s, t8=t8, sl=sl: e.dma_start(
                        hh[sl][:],
                        sc_h.rearrange("i p t -> p i t")[:, :, t8 * TQ:(t8 + 1) * TQ]),
                    waits=waits, dma=HH[sl])

    if T == 2048:
        hh_pre = {0: b.op("sync",
                          lambda e, s: e.dma_start(
                              hh[0][:, 48:64, :],
                              sc_h.rearrange("i p t -> p i t")[:, 48:64, 0:TQ]),
                          waits=sh_tots, dma="HH0")}
        if NT8 > 1:
            hh_pre[1] = load_hh(1, [mm_p4a, vv_p4a] + sh_tots)
        wd_tok[1] = load_wdn(1, [mm_p4a], eng="scalar")
    else:
        hh_pre = {0: load_hh(0, p4a_all)}
        if NT8 > 1:
            hh_pre[1] = load_hh(1, [])
        wd_tok[0] = load_wdn(0, p4a_all, eng="scalar")
        wd_tok[1] = load_wdn(1, [], eng="scalar")

    b.barrier(p4a_end, exclude=("CI",))

    t_cc = b.op("gpsimd",
                lambda e, s: e.collective_compute(
                    "AllGather", ALU.bypass,
                    replica_groups=[list(range(NCORES))],
                    ins=[cc_in[:, :]], outs=[cc_out[:, :, :]]),
                waits=[t_ci], inc=("CCS", 1))
    t_gs = b.op("gpsimd",
                lambda e, s: e.dma_start(gsb[:], cc_out.rearrange("r p c -> p r c")),
                waits=[t_cc], dma=GS)

    q2_mm = [None]
    hh_g = {0: hh_pre.get(0), 1: hh_pre.get(1)}
    slot_prev = [None, None]
    for q in range(4):
        last_mm = None
        for t8 in range(NT8):
            g = q * NT8 + t8
            sl = t8 % 2
            gsl = t8 % 2
            t_hh = hh_g.get(g)
            if t_hh is None:
                t_hh = load_hh(t8, [slot_prev[sl]])
            gops = []
            body_mms = []
            for hc in range(4):
                bk = mmcnt[0] % 4
                mmcnt[0] += 1
                t_mm = None
                for ic in range(NIC):
                    t_mm = b.op("tensor",
                                lambda e, s, bk=bk, ic=ic, hc=hc, sl=sl, q=q: e.matmul(
                                    pb[bk][:, 0:TQ], wdn_q[q % 2][:, ic, hc * 128:(hc + 1) * 128],
                                    hh[sl][:, ic, :],
                                    start=(ic == 0), stop=(ic == NIC - 1)),
                                waits=([wd_tok[q], t_hh, bank_free[bk]] if ic == 0 else []),
                                notok=(ic < NIC - 1))
                last_mm = t_mm
                t_gt = b.op("scalar",
                            lambda e, s, bk=bk, hc=hc, gsl=gsl: e.activation(
                                g_t[gsl][:, hc, :], pb[bk][:, 0:TQ], A.Silu),
                            waits=[t_mm] + (gt_consumer[gsl] if hc == 0 else []))
                if hc == 0:
                    gt_consumer[gsl] = []
                bank_free[bk] = t_gt
                gops.append(t_gt)
                body_mms.append(t_mm)
            slot_prev[sl] = last_mm
            if g + 2 < 4 * NT8:
                nq, nt = divmod(g + 2, NT8)
                hh_g[g + 2] = load_hh(nt, [last_mm])
            t_sg3 = b.op("sync",
                         lambda e, s, q=q, t8=t8, gsl=gsl: e.dma_start(
                             sc_g.rearrange("f p t -> p f t")[:, q * 4:(q + 1) * 4,
                                                              t8 * TQ:(t8 + 1) * TQ],
                             g_t[gsl][:]),
                         waits=gops + [("SGSP" + str(gsl), b.total["SGSP" + str(gsl)])],
                         dma=SGSP[gsl])
            gt_consumer[gsl] = [t_sg3]
        if q + 2 <= 3:
            wd_tok[q + 2] = load_wdn(q + 2, [last_mm], eng="gpsimd")
        if q == 2:
            q2_mm[0] = last_mm

    p4b_end = b.snapshot()
    p4b.close()

    p5 = contextlib.ExitStack()

    def sb5(name, shape, dtype=F32):
        sbufs[name] = p5.enter_context(nc.sbuf_tensor(name, shape, dtype))
        return sbufs[name]

    d5 = sb5("d5", [128, NE, TQ])
    bp5 = sb5("bp5", [128, NE, TQ])
    ga5 = sb5("ga5", [128, NE, TQ])
    gg5 = sb5("gg5", [128, NE, TQ])
    st5 = sb5("st5", [128, NE, TQ])
    y5 = sb5("y5", [128, NE, TQ], BF16)
    s5 = sb5("s5", [128, NE, TQ])
    if T == 2048:
        pad5 = sb5("pad5", [128, 2528], mybir.dt.uint8)
    wout = sb5("wout", [128, NE, D], BF16)
    WO2 = b.sem("WO2")
    wo_all = b.op("sync",
                  lambda e, s: e.dma_start(wout[:], sc_wout.rearrange("k p c -> p k c")),
                  waits=[q2_mm[0] if T == 2048 else ("MM", p4b_end["MM"])], dma=WO2)

    b.barrier(p4b_end)

    x5 = [sb5(f"x5_{i}", [128, D]) for i in range(2)]
    out5 = [sb5(f"out5{i}", [128, D]) for i in range(2)]

    gs_tok = ("GS", b.total["GS"])
    t_h0 = b.op("vector", lambda e, s: e.memset(hcomb[:], 0.0), waits=[gs_tok, l0_tok])
    prev = t_h0
    for jcore in range(NCORES - 1):
        t_a = b.op("vector",
                   lambda e, s, j=jcore: e.tensor_mul(tmp1[:], gsb[:, j, 0:NE], hcomb[:]),
                   waits=[prev])
        t_b2 = b.op("vector",
                    lambda e, s, j=jcore: e.tensor_add(tmp1[:], tmp1[:], gsb[:, j, NE:2 * NE]),
                    waits=[t_a])
        t_c = b.op("vector", lambda e, s: e.tensor_sub(tmp2[:], tmp1[:], hcomb[:]),
                   waits=[t_b2])
        t_d2 = b.op("vector",
                    lambda e, s, j=jcore: e.tensor_scalar_mul(tmp2[:], tmp2[:],
                                                              cmask[:, j:j + 1]),
                    waits=[t_c])
        prev = b.op("vector", lambda e, s: e.tensor_add(hcomb[:], hcomb[:], tmp2[:]),
                    waits=[t_d2])
    t_carry0 = b.op("vector", lambda e, s: e.tensor_copy(carry[:], hcomb[:]), waits=[prev])

    LD5 = b.sem("LD5")
    LB5 = b.sem("LB5")
    LG5 = b.sem("LG5")
    LGG = b.sem("LGG")
    LX5 = [b.sem("LX5a"), b.sem("LX5b")]
    SOUT = [b.sem("SOUT0"), b.sem("SOUT1")]
    y5_cons = []
    out5_cons = [[], []]
    carry_last = [t_carry0]

    def ld_sc(eng, sem, dst, src, t8, waits):
        return b.op(eng,
                    lambda e, s, t8=t8, dst=dst, src=src: e.dma_start(
                        dst[:], src.rearrange("f p t -> p f t")[:, :, t8 * TQ:(t8 + 1) * TQ]),
                    waits=waits, dma=sem)

    t_ld = ld_sc("sync", LD5, d5, sc_d, 0, [])
    t_lb = ld_sc("sync", LB5, bp5, sc_bp, 0, [])
    t_lg = ld_sc("gpsimd", LG5, ga5, sc_gate, 0, [])
    t_lgg = ld_sc("sync", LGG, gg5, sc_g, 0, [])

    t_x5 = [b.op("gpsimd",
                 lambda e, s, j2=j2: e.dma_start(x5[j2][:], x_h[j2 * 128:(j2 + 1) * 128, :]),
                 dma=LX5[j2]) for j2 in range(2)]

    def emit_scans(t8, t_ld, t_lb, first_extra=()):
        sc = []
        prev = carry_last[0]
        for fc in range(NE):
            t_s5 = b.op("vector",
                        lambda e, s, fc=fc, t8=t8: e.tensor_tensor_scan(
                            st5[:, fc, :], d5[:, fc, :], bp5[:, fc, :],
                            carry[:, fc:fc + 1], ALU.mult, ALU.subtract),
                        waits=([t_ld, t_lb, prev] + list(first_extra)) if fc == 0
                              else [sc[fc - 1]])
            sc.append(t_s5)
        t_cy = b.op("vector",
                    lambda e, s: e.tensor_copy(
                        carry[:, 0:NE], st5[:, :, TQ - 1:TQ].rearrange("p f o -> p (f o)")),
                    waits=[sc[-1]])
        carry_last[0] = t_cy
        return sc

    scans = emit_scans(0, t_ld, t_lb)
    pend = None
    for t8 in range(NT8):
        t_y = b.op("vector", lambda e, s: e.tensor_mul(y5[:], ga5[:], st5[:]),
                   waits=[t_lg] + scans + y5_cons)
        y5_cons = []
        if t8 + 1 < NT8:
            t_ldn = ld_sc("sync", LD5, d5, sc_d, t8 + 1, [scans[-1]])
            t_lbn = ld_sc("sync", LB5, bp5, sc_bp, t8 + 1, [scans[-1]])
            t_lgn = ld_sc("gpsimd", LG5, ga5, sc_gate, t8 + 1, [t_y])
            scans = emit_scans(t8 + 1, t_ldn, t_lbn, first_extra=[t_y])
        if pend is not None:
            pt8, pcps, px5 = pend
            adds = []
            for j2 in range(TQ // 128):
                row0 = pt8 * TQ + j2 * 128
                t_ad = b.op("vector",
                            lambda e, s, j2=j2: e.tensor_add(
                                out5[j2][:], out5[j2][:], x5[j2][:]),
                            waits=pcps[j2] + [px5[j2]])
                adds.append(t_ad)
                t_st = b.op("sync",
                            lambda e, s, row0=row0, j2=j2: e.dma_start(
                                out_h[row0:row0 + 128, :], out5[j2][:]),
                            waits=[t_ad], dma=SOUT[j2])
                out5_cons[j2] = [t_st]
            t_x5 = [b.op("gpsimd",
                         lambda e, s, j2=j2, t8=t8: e.dma_start(
                             x5[j2][:], x_h[t8 * TQ + j2 * 128:t8 * TQ + (j2 + 1) * 128, :]),
                         waits=[adds[j2]], dma=LX5[j2])
                    for j2 in range(2)]
        ev5 = []
        for e_ in range(NE):
            bk = e_ % 2
            t_mm = None
            for k in range(NE):
                t_mm = b.op("tensor",
                            lambda e, s, bk=bk, k=k, e_=e_: e.matmul(
                                pb[bk][:, 0:TQ], wout[:, k, e_ * 128:(e_ + 1) * 128],
                                y5[:, k, :], start=(k == 0), stop=(k == NE - 1)),
                            waits=([wo_all, t_y, bank_free[bk]]
                                   if k == 0 else []),
                            notok=(k < NE - 1))
            y5_cons = [t_mm]
            t_e5 = b.op("scalar",
                        lambda e, s, bk=bk, e_=e_: e.activation(
                            s5[:, e_, :], pb[bk][:, 0:TQ], A.Identity,
                            bias=bo_t[:, e_:e_ + 1]),
                        waits=[t_mm])
            bank_free[bk] = t_e5
            ev5.append(t_e5)
        t_rg = []
        for g4 in range(4):
            t_rg.append(b.op("vector",
                             lambda e, s, g4=g4: e.tensor_mul(
                                 s5[:, g4 * 4:(g4 + 1) * 4, :],
                                 s5[:, g4 * 4:(g4 + 1) * 4, :],
                                 gg5[:, g4 * 4:(g4 + 1) * 4, :]),
                             waits=[ev5[g4 * 4 + 3], t_lgg] if g4 == 0
                                   else [ev5[g4 * 4 + 3]]))
        if t8 + 1 < NT8:
            t_lggn = ld_sc("sync", LGG, gg5, sc_g, t8 + 1, [t_rg[-1]])
        cps = {0: [], 1: []}
        for j2 in range(TQ // 128):
            for grp in range(4):
                bk = (j2 * 4 + grp) % 2
                t_tr = None
                for qq in range(4):
                    fc = grp * 4 + qq
                    t_tr = b.op("tensor",
                                lambda e, s, bk=bk, qq=qq, fc=fc, j2=j2: e.transpose(
                                    banks6[4 + bk][:, qq * 128:(qq + 1) * 128],
                                    s5[:, fc, j2 * 128:(j2 + 1) * 128], eye32[:]),
                                waits=([t_rg[grp], bank_free[4 + bk], l0_tok]
                                       if qq == 0 else []),
                                notok=(qq < 3))
                t_cp = b.op("scalar",
                            lambda e, s, bk=bk, grp=grp, j2=j2: e.copy(
                                out5[j2][:, grp * 512:(grp + 1) * 512],
                                banks6[4 + bk][:, 0:512]),
                            waits=[t_tr] + (out5_cons[j2] if grp == 0 else []))
                if grp == 0:
                    out5_cons[j2] = []
                bank_free[4 + bk] = t_cp
                cps[j2].append(t_cp)
        pend = (t8, cps, t_x5)
        if t8 + 1 < NT8:
            t_ld, t_lb, t_lg, t_lgg = t_ldn, t_lbn, t_lgn, t_lggn

    pt8, pcps, px5 = pend
    for j2 in range(TQ // 128):
        row0 = pt8 * TQ + j2 * 128
        t_ad = b.op("vector",
                    lambda e, s, j2=j2: e.tensor_add(out5[j2][:], out5[j2][:], x5[j2][:]),
                    waits=pcps[j2] + [px5[j2]])
        t_st = b.op("sync",
                    lambda e, s, row0=row0, j2=j2: e.dma_start(
                        out_h[row0:row0 + 128, :], out5[j2][:]),
                    waits=[t_ad], dma=SOUT[j2])
        out5_cons[j2] = [t_st]

    b.wait("sync", [("SOUT0", b.total["SOUT0"]), ("SOUT1", b.total["SOUT1"])])

    sem_handles = {}
    for name in b.sem_names:
        sem_handles[name] = es.enter_context(nc.semaphore(name))
    with nc.Block() as blk:
        @blk.tensor
        def _(e):
            b.replay("tensor", e, sem_handles)

        @blk.scalar
        def _(e):
            b.replay("scalar", e, sem_handles)

        @blk.vector
        def _(e):
            b.replay("vector", e, sem_handles)

        @blk.gpsimd
        def _(e):
            b.replay("gpsimd", e, sem_handles)

        @blk.sync
        def _(e):
            b.replay("sync", e, sem_handles)
    p5.close()
    es.close()
    return nc


_NC_CACHE = {}


def _get_nc(T):
    if T not in _NC_CACHE:
        _NC_CACHE[T] = build_nc(T)
    return _NC_CACHE[T]


def _bias_tab(v):
    return np.ascontiguousarray(np.asarray(v, dtype=np.float32).reshape(NE, 128).T)


def _attn_blocks(w):
    w = np.asarray(w, dtype=np.float32)
    return np.ascontiguousarray(w.reshape(NE, 128, NE, 128).transpose(2, 1, 0, 3))


def kernel(L0_attn_x, L0_attn_W_in, L0_attn_b_in, L0_attn_W_decay, L0_attn_b_decay,
           L0_attn_W_gate, L0_attn_b_gate, L0_attn_W_out, L0_attn_b_out,
           L0_attn_decay_bias, L0_ffn_w_gate_up, L0_ffn_w_down):
    T = SEQ // NCORES
    x = np.ascontiguousarray(np.asarray(L0_attn_x, dtype=np.float32).reshape(SEQ, D))
    w_gu = np.asarray(L0_ffn_w_gate_up, dtype=np.float32)
    w_gu_r = np.ascontiguousarray(
        w_gu.reshape(2 * NIC, 128, NE, 128).transpose(0, 3, 2, 1))
    w_dn = np.asarray(L0_ffn_w_down, dtype=np.float32)
    w_dn_r = np.ascontiguousarray(
        w_dn.T.reshape(4, 16, 128, 4, 512).transpose(3, 0, 2, 1, 4))
    w_out = np.ascontiguousarray(np.asarray(L0_attn_W_out, dtype=np.float32))
    base = dict(
        w_in_r=_attn_blocks(L0_attn_W_in), w_dec_r=_attn_blocks(L0_attn_W_decay),
        w_gate_r=_attn_blocks(L0_attn_W_gate), w_out=w_out,
        w_gu_r=w_gu_r, w_dn_r=w_dn_r,
        b_in_t=_bias_tab(L0_attn_b_in), b_dec_t=_bias_tab(L0_attn_b_decay),
        b_db_t=_bias_tab(L0_attn_decay_bias), b_gate_t=_bias_tab(L0_attn_b_gate),
        b_out_t=_bias_tab(L0_attn_b_out),
        eye32=np.eye(128, dtype=np.float32),
    )
    in_maps = []
    for c in range(NCORES):
        m = dict(base)
        m["x"] = np.ascontiguousarray(x[c * T:(c + 1) * T, :])
        cm = np.zeros((128, NCORES), dtype=np.float32)
        cm[:, :c] = 1.0
        m["cmask"] = cm
        in_maps.append(m)
    nc = _get_nc(T)
    res = run_bass_kernel_spmd(nc, in_maps, core_ids=list(range(NCORES)))
    out = np.concatenate([np.asarray(r["out"]) for r in res.results], axis=0)
    return out.reshape(1, SEQ, D).astype(np.float32)
